# version 15
# speedup vs baseline: 1.0643x; 1.0061x over previous
import numpy as np
import concourse.bass as bass
import concourse.mybir as mybir
from concourse.bass_utils import run_bass_kernel_spmd
from contextlib import ExitStack

F32 = mybir.dt.float32
BF16 = mybir.dt.bfloat16
F8 = mybir.dt.float8e4
I32 = mybir.dt.int32
AF = mybir.ActivationFunctionType
ALU = mybir.AluOpType
AX = mybir.AxisListType

S = 2048
D = 1024
NH = 16
DH = 64
NC = 8
NTT = 16
EPS = 1e-5
ENGINES = ("tensor", "vector", "scalar", "gpsimd", "sync")
R224 = 224.0
INV224 = float(np.float32(1.0 / 224.0))
INV7 = float(np.float32(1.0 / 7.0))
INVD = float(np.float32(1.0 / 1024.0))
STAGE_FINAL = 99


def _hadamard(n):
    h = np.array([[1.0]], dtype=np.float64)
    while h.shape[0] < n:
        h = np.block([[h, h], [h, -h]])
    return (h / np.sqrt(n)).astype(np.float32)


def make_consts():
    c = {}
    c["c_identf"] = np.eye(128, dtype=np.float32)
    c["c_identb"] = np.eye(128, dtype=np.float32)
    c["c_onesrow"] = np.ones((1, 128), dtype=np.float32)
    c["c_onescol"] = np.ones((128, 1), dtype=np.float32)
    c["c_ones128"] = np.ones((128, 128), dtype=np.float32)
    c["c_iotac"] = np.arange(128, dtype=np.float32).reshape(128, 1)
    c["c_iotar"] = np.broadcast_to(np.arange(1024, dtype=np.float32),
                                   (128, 1024)).copy()
    c["c_hadam"] = _hadamard(D)
    kr = np.arange(128)[:, None]
    qf = np.arange(512)[None, :]
    masks = [(np.abs(off + kr - qf) <= 256).astype(np.float32)
             for off in range(-256, 768, 128)]
    c["c_lmasks"] = np.concatenate(masks, axis=1)
    c["c_tri"] = (np.arange(128)[:, None] < np.arange(128)[None, :]).astype(
        np.float32)
    return c


def colchunks(w, nblocks):
    return np.ascontiguousarray(
        w.reshape(8, 128, nblocks, 128).transpose(2, 1, 0, 3)
        .reshape(nblocks, 128, 1024))


class PB:
    def __init__(self, nc, ctx):
        self.nc = nc
        self.ctx = ctx
        self.q = {e: [] for e in ENGINES}
        self.sems = {}
        self.cnt = {}

    def sem(self, name):
        if name not in self.sems:
            self.sems[name] = self.ctx.enter_context(self.nc.semaphore(name))
            self.cnt[name] = 0
        return self.sems[name]

    def val(self, name):
        self.sem(name)
        return self.cnt[name]

    def bump(self, name, by=1):
        self.sem(name)
        self.cnt[name] += by
        return self.cnt[name]

    def op(self, eng, meth, *args, waits=(), inc=None, drain=False, **kw):
        import sys as _sys
        fr = _sys._getframe(1)
        tag = f"{fr.f_code.co_name}:{fr.f_lineno}"
        fr2 = fr.f_back
        if fr2 is not None:
            tag += f"<{fr2.f_code.co_name}:{fr2.f_lineno}"
        wl = [(self.sem(s), v) for (s, v) in waits]
        ip = None
        if inc is not None:
            s, by = inc
            self.bump(s, by)
            ip = (self.sems[s], by)

        def fn(e, meth=meth, args=args, kw=kw, wl=wl, ip=ip, drain=drain,
               tag=tag):
            for s, v in wl:
                e.wait_ge(s, v)
            if drain:
                e.drain()
            ins = getattr(e, meth)(*args, **kw)
            ins.annotate(tag)
            if ip is not None:
                ins.then_inc(*ip)
        self.q[eng].append(fn)

    def wait(self, eng, name, v):
        s = self.sem(name)
        self.q[eng].append(lambda e, s=s, v=v: e.wait_ge(s, v))

    def barrier(self, engines=ENGINES):
        targets = {}
        for e in engines:
            targets[e] = self.bump("bar_" + e)
        for e in engines:
            s = self.sems["bar_" + e]

            def fin(eng, s=s):
                eng.drain().then_inc(s, 1)
            self.q[e].append(fin)
        for e in engines:
            for o in engines:
                if o != e:
                    self.wait(e, "bar_" + o, targets[o])

    def finalize(self, block):
        for e in ENGINES:
            fns = self.q[e]

            def body(eng, fns=fns):
                for f in fns:
                    f(eng)
            getattr(block, e)(body)


def ap3(t, off, dims):
    return bass.AP(t, off, dims)


def build_program(stage=STAGE_FINAL, debug_taps=()):
    nc = bass.Bass("TRN2", target_bir_lowering=False)
    taps = set(debug_taps)

    def din(name, shape, dt=F32):
        return nc.dram_tensor(name, list(shape), dt, kind="ExternalInput").ap()

    x_in = din("x_in", (S, D))
    wqkv_r = din("wqkv_r", (24, 128, 1024))
    bqkv_cols = din("bqkv_cols", (128, 24))
    wout_r = din("wout_r", (8, 128, 1024))
    bout_cols = din("bout_cols", (128, 8))
    w_ffnT = din("w_ffnT", (D, D))
    w_ffn = din("w_ffn", (D, D))
    l1w_r = {k: din(f"l1{k}_r", (8, 128, 1024)) for k in ("wq", "wk", "wv", "wo")}
    bcols_in = {k: din(f"b{k}_cols", (128, 8)) for k in ("q", "k", "v", "o")}
    wg_cols = din("wg_cols", (128, 8))
    fw_r = {k: din(f"f{k}_r", (8, 128, 1024)) for k in ("Wq", "Wk", "Wv")}
    fWo_nat = din("fWo_nat", (D, D))
    c_identf = din("c_identf", (128, 128))
    c_identb = din("c_identb", (128, 128))
    c_onesrow = din("c_onesrow", (1, 128))
    c_onescol = din("c_onescol", (128, 1))
    c_ones128 = din("c_ones128", (128, 128))
    c_tri = din("c_tri", (128, 128))
    c_iotac = din("c_iotac", (128, 1))
    c_iotar = din("c_iotar", (128, 1024))
    c_hadam = din("c_hadam", (D, D))
    c_lmasks = din("c_lmasks", (128, 8 * 512))

    out_t = nc.dram_tensor("out", [S, D], F32, kind="ExternalOutput").ap()

    dbg = {}

    def mkdbg(name, shape):
        if name in taps:
            dbg[name] = nc.dram_tensor("dbg_" + name, list(shape), F32,
                                       kind="ExternalOutput").ap()
    for nm, shp in [("xT", (128, NC * S)), ("xn", (128, NC * S)),
                    ("khat", (128, NC * S)), ("srow", (1, S)),
                    ("x_ffn", (128, NC * S)), ("xdq", (128, NC * S)),
                    ("qT", (128, 4 * S)), ("kT", (128, 4 * S)),
                    ("vaug", (128, NTT * 8 * 65)), ("e0", (128, 1024)),
                    ("odq", (128, NC * S)), ("x_l0", (128, NC * S)),
                    ("xn2", (128, NC * S)), ("lrow", (1, S)),
                    ("locol", (128, 1)), ("rankp", (128, 16)),
                    ("attn_in", (128, 8 * 1024)), ("lsel", (1, 1024)),
                    ("y2g", (128, 8 * 1024)), ("x_ffn2", (128, NC * S)),
                    ("o2", (128, NC * S)), ("x_l1", (128, NC * S)),
                    ("tern", (128, NC * 1024)), ("socol", (128, 8)),
                    ("gcols", (128, 8)), ("selp", (128, 16))]:
        mkdbg(nm, shp)

    with ExitStack() as ctx:
        p = PB(nc, ctx)

        def sbt(name, shape, dt=F32):
            return ctx.enter_context(nc.sbuf_tensor(name, list(shape), dt))

        AXr = sbt("AX", (128, 16384))
        A1 = sbt("A1", (128, 16384))
        A2 = sbt("A2", (128, 16384))
        identf = sbt("identf", (128, 128))
        identb = sbt("identb", (128, 128), BF16)
        onesrow = sbt("onesrow", (1, 128))
        onescol = sbt("onescol", (128, 1))
        tri_b = sbt("tri_b", (128, 128), BF16)
        tri_f = sbt("tri_f", (128, 128))
        iotac = sbt("iotac", (128, 1))
        iotar = sbt("iotar", (128, 1024))
        bqkv = sbt("bqkv", (128, 24))
        bout = sbt("bout", (128, 8))
        bcol = {k: sbt("b_" + k, (128, 8)) for k in ("q", "k", "v", "o")}
        wg = sbt("wg", (128, 8))
        wgh = sbt("wgh", (128, 8), BF16)
        wgl = sbt("wgl", (128, 8), BF16)
        scl = sbt("scl", (128, 64))
        sclb = scl[:].bitcast(BF16)
        ebuf = sbt("ebuf", (128, 2048), BF16)
        wqd = [sbt("wqd0", (128, 1024), BF16), sbt("wqd1", (128, 1024), BF16)]
        w8 = sbt("w8", (128, 1024), F8)

        psA = ctx.enter_context(nc.psum_tensor("psA", [128, 2048], F32))
        psB = ctx.enter_context(nc.psum_tensor("psB", [128, 2048], F32))

        block = ctx.enter_context(nc.Block())

        A1b = A1[:].bitcast(BF16)
        A2b = A2[:].bitcast(BF16)
        A2f8 = A2[:].bitcast(F8)
        wldf = [A2[:, 12352:13376], A2[:, 13376:14400]]
        hbuf = wqd

        def xc(c):
            return AXr[:, 2048 * c:2048 * (c + 1)]

        def a1f(c):
            return A1[:, 2048 * c:2048 * (c + 1)]

        def bc_row(dst_ps, row_ap, waits=(), inc=None):
            n = row_ap.shape[-1]
            nseg = (n + 511) // 512
            for s in range(nseg):
                a, b = 512 * s, min(512 * (s + 1), n)
                p.op("tensor", "matmul", dst_ps[:, a:b], onesrow[0:1, :],
                     row_ap[0:1, a:b], start=True, stop=True,
                     waits=waits if s == 0 else (),
                     inc=inc if s == nseg - 1 else None)

        def flat_dma(dst_off, src_ap, ncols, waits=()):
            for c in range(ncols):
                p.op("sync", "dma_start",
                     out=ap3(A2, dst_off + 128 * c, [[16384, 1], [1, 128]]),
                     in_=src_ap[:, c:c + 1], waits=waits if c == 0 else (),
                     inc=("dF", 16))

        def transpose_tiles(pairs, ident, tsem, vsem, first_waits=()):
            bf = pairs[0][0].dtype == BF16
            nw = (len(pairs) + 7) // 8
            for wv in range(nw):
                chunk = pairs[8 * wv:8 * wv + 8]
                pst = psA if wv % 2 == 0 else psB
                pstv = pst[:].bitcast(BF16) if bf else pst
                base = 2048 if bf else 1024
                w = [(vsem, p.val(vsem) - 1)] if wv >= 2 else []
                if wv == 0:
                    w = w + list(first_waits)
                for j, (sap, dap) in enumerate(chunk):
                    pt = pstv[:, base + 128 * j:base + 128 * j + 128]
                    p.op("tensor", "transpose", pt, sap, ident,
                         waits=w if j == 0 else [],
                         inc=(tsem, 1) if j == len(chunk) - 1 else None)
                tv = p.val(tsem)
                j0 = 0
                while j0 < len(chunk):
                    j1 = j0
                    while (j1 + 1 < len(chunk)
                           and chunk[j0][1].ndim == 2
                           and chunk[j1 + 1][1].ndim == 2
                           and chunk[j1 + 1][1].tensor is chunk[j0][1].tensor
                           and chunk[j1 + 1][1].offset
                           == chunk[j0][1].offset + 128 * (j1 + 1 - j0)):
                        j1 += 1
                    n = j1 - j0 + 1
                    dst = chunk[j0][1]
                    if n == 1:
                        dsta = dst
                    else:
                        dsta = bass.AP(dst.tensor, dst.offset,
                                       [list(dst.ap[0]), [1, 128 * n]])
                    p.op("vector", "tensor_copy", dsta,
                         pstv[:, base + 128 * j0:base + 128 * j0 + 128 * n],
                         waits=[(tsem, tv)] if j0 == 0 else [],
                         inc=(vsem, 1) if j1 == len(chunk) - 1 else None)
                    j0 = j1 + 1

        def transpose_reduce(srcs, dst_cols_fn, tsem, vsem, first_waits=()):
            nw = (len(srcs) + 7) // 8
            for wv in range(nw):
                chunk = srcs[8 * wv:8 * wv + 8]
                pst = psA if wv % 2 == 0 else psB
                w = [(vsem, p.val(vsem) - 1)] if wv >= 2 else []
                if wv == 0:
                    w = w + list(first_waits)
                for j, sap in enumerate(chunk):
                    pt = pst[:, 1024 + 128 * j:1024 + 128 * j + 128]
                    p.op("tensor", "transpose", pt, sap, identf[:, :],
                         waits=w if j == 0 else [],
                         inc=(tsem, 1) if j == len(chunk) - 1 else None)
                nj = len(chunk)
                p.op("vector", "reduce_max", dst_cols_fn(wv),
                     pst[:, 1024:1024 + 128 * nj].rearrange(
                         "p (j c) -> p j c", j=nj),
                     axis=AX.X, apply_absolute_value=True,
                     waits=[(tsem, p.val(tsem))],
                     inc=(vsem, 1))

        def dump_chunks(name, chfn, n=NC, width=2048):
            if name in taps:
                p.barrier()
                for c2 in range(n):
                    p.op("gpsimd", "dma_start",
                         out=dbg[name][:, width * c2:width * (c2 + 1)],
                         in_=chfn(c2), waits=[("dDbg", p.val("dDbg"))],
                         inc=("dDbg", 16))
                p.wait("gpsimd", "dDbg", p.val("dDbg"))
                p.barrier()

        for dst, src in [(identf, c_identf), (onesrow, c_onesrow),
                         (onescol, c_onescol), (iotac, c_iotac),
                         (iotar, c_iotar), (bqkv, bqkv_cols), (bout, bout_cols),
                         (bcol["q"], bcols_in["q"]), (bcol["k"], bcols_in["k"]),
                         (bcol["v"], bcols_in["v"]), (bcol["o"], bcols_in["o"]),
                         (wg, wg_cols)]:
            p.op("sync", "dma_start", out=dst[:], in_=src[:, :], inc=("dIn", 16))
        p.op("gpsimd", "dma_start", out=identb[:], in_=c_identb[:, :],
             inc=("dInG", 16))
        p.op("gpsimd", "dma_start", out=tri_b[:], in_=c_tri[:, :],
             inc=("dInG", 16))
        p.op("sync", "dma_start", out=tri_f[:], in_=c_tri[:, :],
             inc=("dIn", 16))
        for tt in range(NTT):
            p.op("sync", "dma_start", out=A1[:, 1024 * tt:1024 * (tt + 1)],
                 in_=x_in[128 * tt:128 * (tt + 1), :], inc=("dIn", 16))
        p.wait("tensor", "dIn", p.val("dIn"))
        p.wait("vector", "dIn", p.val("dIn"))
        p.wait("vector", "dInG", p.val("dInG"))
        p.wait("scalar", "dIn", p.val("dIn"))

        for ic in range(NC):
            if ic < 4:
                hdst = A2b[:, 20480 + 1024 * ic:20480 + 1024 * (ic + 1)]
            elif ic < 6:
                hdst = ebuf[:, 1024 * (ic - 4):1024 * (ic - 3)]
            else:
                hdst = wqd[ic - 6][:]
            p.op("gpsimd", "dma_start", out=hdst,
                 in_=c_hadam[128 * ic:128 * (ic + 1), :], inc=("dH0", 16))
        p.op("vector", "tensor_copy", wgh[:], wg[:])
        p.op("vector", "tensor_tensor", wgl[:], wg[:], wgh[:], ALU.subtract,
             drain=True)

        pairs = []
        for fc in range(NC):
            for tt in range(NTT):
                src = A1[:, 1024 * tt + 128 * fc:1024 * tt + 128 * fc + 128]
                dst = xc(fc)[:, 128 * tt:128 * tt + 128]
                pairs.append((src, dst))
        transpose_tiles(pairs, identf[:, :], "tT", "tV")
        p.barrier()
        dump_chunks("xT", xc)

        def emit_layernorm(dst_fn, rows_off, sq_off):
            r_row = A2[0:1, rows_off:rows_off + 2048]
            a_row = A2[0:1, rows_off + 2048:rows_off + 4096]
            m_row = A2[0:1, rows_off + 4096:rows_off + 6144]
            t_row = A2[0:1, rows_off + 6144:rows_off + 8192]
            for c in range(NC):
                for g in range(4):
                    p.op("tensor", "matmul", psA[0:1, 512 * g:512 * (g + 1)],
                         onescol[:, 0:1], xc(c)[:, 512 * g:512 * (g + 1)],
                         start=(c == 0), stop=(c == NC - 1),
                         inc=("lnt", 1) if (c == NC - 1 and g == 3) else None)
            for c in range(NC):
                slot = c % 2
                sq = A2[:, sq_off + 2048 * slot: sq_off + 2048 * (slot + 1)]
                w = [("lnt2", p.val("lnt2") - 1)] if c >= 2 else []
                p.op("scalar", "activation", sq, xc(c), AF.Square, drain=True,
                     waits=w, inc=("lns", 1))
                for g in range(4):
                    p.op("tensor", "matmul", psB[0:1, 512 * g:512 * (g + 1)],
                         onescol[:, 0:1], sq[:, 512 * g:512 * (g + 1)],
                         start=(c == 0), stop=(c == NC - 1),
                         waits=[("lns", p.val("lns"))] if g == 0 else [],
                         inc=("lnt2", 1) if g == 3 else None)
            p.op("vector", "tensor_single_scalar", m_row, psA[0:1, :], INVD,
                 ALU.mult, waits=[("lnt", p.val("lnt")),
                                  ("lnt2", p.val("lnt2"))], drain=True)
            p.op("vector", "tensor_single_scalar", r_row, psB[0:1, :], INVD,
                 ALU.mult, drain=True)
            p.op("vector", "tensor_mul", a_row, m_row, m_row, drain=True)
            p.op("vector", "tensor_tensor", r_row, r_row, a_row, ALU.subtract,
                 drain=True)
            p.op("vector", "tensor_single_scalar", r_row, r_row, EPS, ALU.add,
                 drain=True, inc=("lnv", 1))
            p.op("scalar", "activation", a_row, r_row, AF.Ln,
                 waits=[("lnv", p.val("lnv"))], drain=True)
            p.op("scalar", "activation", a_row, a_row, AF.Exp, scale=-0.5,
                 drain=True, inc=("lns", 1))
            p.op("vector", "tensor_mul", t_row, a_row, a_row,
                 waits=[("lns", p.val("lns"))], drain=True)
            p.op("vector", "tensor_mul", t_row, t_row, r_row, drain=True)
            p.op("vector", "tensor_scalar", t_row, t_row, -0.5, 1.5,
                 op0=ALU.mult, op1=ALU.add, drain=True)
            p.op("vector", "tensor_mul", r_row, a_row, t_row, drain=True)
            p.op("vector", "tensor_mul", a_row, m_row, r_row, drain=True)
            p.op("vector", "tensor_single_scalar", a_row, a_row, -1.0, ALU.mult,
                 drain=True, inc=("lnv", 1))
            p.barrier(("tensor", "vector"))
            for half in range(2):
                cols = slice(1024 * half, 1024 * (half + 1))
                bc_row(psA[:, 0:1024], r_row[0:1, cols])
                bc_row(psA[:, 1024:2048], a_row[0:1, cols], inc=("lnt", 1))
                for c in range(NC):
                    dst = dst_fn(c)[:, cols]
                    p.op("vector", "tensor_tensor", dst, xc(c)[:, cols],
                         psA[:, 0:1024], ALU.mult,
                         waits=[("lnt", p.val("lnt"))] if c == 0 else [],
                         drain=(c == 0))
                    p.op("vector", "tensor_add", dst, dst, psA[:, 1024:2048],
                         drain=True)
                p.barrier(("tensor", "vector"))

        emit_layernorm(a1f, rows_off=0, sq_off=12288)
        p.barrier()
        dump_chunks("xn", a1f)
        if stage <= 1:
            _emit_out_zero(p, out_t, A2)
            p.finalize(block)
            return nc, dbg

        xh = A2[:, 8192:10240]
        srow = A2[0:1, 10240:12288]
        rrow = A2[0:1, 12544:12800]
        hi_g = A2b[:, 2 * 12800:2 * 12800 + 2048]
        lo_g = A2b[:, 2 * 13824:2 * 13824 + 2048]
        sgn = A2b[:, 29696:31744]

        def khat_ch(c):
            return A2b[:, 2048 * c:2048 * (c + 1)]

        def psbank(i, n=512):
            return (psA if i < 4 else psB)[:, 512 * (i % 4):512 * (i % 4) + n]

        a1_3 = A1[:, 0:16384].rearrange("p (c t) -> p c t", c=8)
        hi3 = hi_g[:, 0:2048].rearrange("p (c t) -> p c t", c=8)
        lo3 = lo_g[:, 0:2048].rearrange("p (c t) -> p c t", c=8)
        kh3 = A2b[:, 0:16384].rearrange("p (c t) -> p c t", c=8)
        usl3 = xh[:, 0:2048].rearrange("p (o t) -> p o t", o=8)
        sgn3 = sgn[:, 0:2048].rearrange("p (o t) -> p o t", o=8)
        xh3 = ap3(A2, 8192, [[16384, 128], [256, 8], [1, 256]])
        rbc = ap3(psB, 256, [[2048, 128], [0, 8], [1, 256]])

        def hslab(ic):
            if ic < 4:
                return A2b[:, 20480 + 1024 * ic:20480 + 1024 * (ic + 1)]
            if ic < 6:
                return ebuf[:, 1024 * (ic - 4):1024 * (ic - 3)]
            return wqd[ic - 6][:]

        ffv_hl = {}
        ffv_x = {}
        for g in range(9):
            if g >= 1:
                gp = g - 1
                ew = [("ffLd", 8 * g)]
                if gp >= 1:
                    ew.append(("ffU", 2 * gp))
                    ew.append(("ffT2", p.val("ffT2")))
                p.op("scalar", "activation", xh, psA[:, 0:2048], AF.Copy,
                     waits=ew, drain=True, inc=("ffE2", 1))
                for wv in range(2):
                    for j in range(NC):
                        w = []
                        if wv == 0 and j == 0:
                            w.append(("ffE2", g))
                            if gp >= 1:
                                w.append(("ffG", gp))
                                w.append(("ffU", 2 * gp))
                        p.op("tensor", "transpose",
                             psB[:, 1024 * wv + 128 * j:
                                 1024 * wv + 128 * j + 128],
                             xh[:, 256 * j + 128 * wv:256 * j + 128 * wv + 128],
                             identf[:, :], waits=w,
                             inc=("ffT2", 1) if j == NC - 1 else None)
                for wv in range(2):
                    p.op("vector", "reduce_max",
                         scl[:, 16 + 8 * wv:24 + 8 * wv],
                         psB[:, 1024 * wv:1024 * wv + 1024].rearrange(
                             "p (j c) -> p j c", j=8),
                         axis=AX.X, apply_absolute_value=True,
                         waits=[("ffT2", p.val("ffT2"))] if wv == 0 else [],
                         drain=(wv == 0))
                for sub in range(2):
                    p.op("vector", "reduce_max", scl[:, sub:sub + 1],
                         scl[:, 16 + 8 * sub:24 + 8 * sub], axis=AX.X,
                         drain=(sub == 0),
                         inc=("ffG", 1) if sub == 1 else None)
                p.op("vector", "tensor_single_scalar", scl[:, 0:2],
                     scl[:, 0:2], 1e-5, ALU.max, drain=True)
                p.op("vector", "reciprocal", scl[:, 2:4], scl[:, 0:2],
                     drain=True)
                p.op("vector", "tensor_single_scalar", scl[:, 2:4],
                     scl[:, 2:4], 7.0, ALU.mult, drain=True)
                p.op("vector", "tensor_single_scalar",
                     scl[:, 32 + 2 * gp:34 + 2 * gp], scl[:, 0:2], INV7,
                     ALU.mult, inc=("ffV", 1))
                for j in range(2):
                    p.op("tensor", "transpose",
                         psB[0:1, 128 * j:128 * j + 128],
                         scl[:, 2 + j:3 + j], identf[:, :],
                         waits=[("ffV", p.val("ffV"))] if j == 0 else [],
                         inc=("ffT3", 1) if j == 1 else None)
                p.op("vector", "tensor_copy", rrow, psB[0:1, 0:256],
                     waits=[("ffT3", p.val("ffT3"))], drain=True,
                     inc=("ffV", 1))
                bc_row(psB[:, 256:512], rrow[0:1, :],
                       waits=[("ffV", p.val("ffV"))], inc=("ffT3", 1))
            if g < 8:
                tgn = slice(256 * g, 256 * (g + 1))
                w = [("ffLd", 8 * g)] if g > 0 else []
                p.op("vector", "tensor_copy", hi3, a1_3[:, :, tgn],
                     drain=True, waits=w)
                p.op("vector", "tensor_tensor", lo3, a1_3[:, :, tgn], hi3,
                     ALU.subtract, drain=True, inc=("ffV", 1))
                ffv_hl[g] = p.val("ffV")
                for oc in range(NC):
                    dst = psA[:, 256 * oc:256 * (oc + 1)]
                    for ic in range(NC):
                        buf = hslab(ic)
                        w = []
                        if oc == 0 and ic == 0:
                            w.append(("ffV", ffv_hl[g]))
                            if g >= 1:
                                w.append(("ffE2", g))
                            if g == 0:
                                w.append(("dH0", p.val("dH0")))
                        p.op("tensor", "matmul", dst,
                             buf[:, 128 * oc:128 * (oc + 1)],
                             hi_g[:, 256 * ic:256 * (ic + 1)],
                             start=(ic == 0), stop=False, waits=w)
                        p.op("tensor", "matmul", dst,
                             buf[:, 128 * oc:128 * (oc + 1)],
                             lo_g[:, 256 * ic:256 * (ic + 1)],
                             start=False, stop=(ic == NC - 1),
                             inc=("ffLd", 1) if ic == NC - 1 else None)
            if g >= 1:
                gp = g - 1
                tg = slice(256 * gp, 256 * (gp + 1))
                rb4 = ap3(psB, 256, [[2048, 128], [0, 4], [1, 256]])
                u_h = A2[:, 14848:15872]
                sg_h = A2b[:, 31744:32768]
                u3 = u_h.rearrange("p (o t) -> p o t", o=4)
                sg3h = sg_h.rearrange("p (o t) -> p o t", o=4)
                for hh in range(2):
                    eng = "vector"
                    w0 = []
                    if hh == 0:
                        w0.append(("ffT3", p.val("ffT3")))
                        if gp >= 1:
                            w0.append(("ffK", gp))
                    p.op("vector", "tensor_tensor", u3,
                         ap3(A2, 8192 + 1024 * hh,
                             [[16384, 128], [256, 4], [1, 256]]),
                         rb4, ALU.mult,
                         waits=w0, drain=True, inc=("ffU", 1))
                    p.op("scalar", "activation", sg_h, u_h, AF.Sign,
                         waits=[("ffU", p.val("ffU"))], drain=True)
                    p.op("scalar", "activation", u_h, u_h, AF.Abs,
                         drain=True, inc=("ffS", 1))
                    ksl3 = kh3[:, 4 * hh:4 * hh + 4, tg]
                    fw = [("ffS", p.val("ffS"))]
                    p.op(eng, "tensor_single_scalar", ksl3, u3, 0.5,
                         ALU.is_ge, drain=True, waits=fw)
                    for k in range(1, 7):
                        p.op(eng, "scalar_tensor_tensor", ksl3, u3,
                             k + 0.5, ksl3, op0=ALU.is_ge, op1=ALU.add,
                             drain=True)
                    p.op(eng, "tensor_tensor", ksl3, ksl3, sg3h,
                         ALU.mult, drain=True,
                         inc=("ffK", 1) if hh == 1 else None)
        p.barrier()

        dump_chunks("khat", khat_ch)
        if "srow" in taps:
            p.op("gpsimd", "dma_start", out=dbg["srow"][:, :], in_=srow,
                 waits=[("dDbg", p.val("dDbg"))], inc=("dDbg", 16))
            p.wait("gpsimd", "dDbg", p.val("dDbg"))
            p.barrier()

        for j in range(16):
            pst = psA if j < 8 else psB
            p.op("tensor", "transpose",
                 pst[0:1, 1024 + 128 * (j % 8):1152 + 128 * (j % 8)],
                 scl[:, 32 + j:33 + j], identf[:, :],
                 inc=("ffT4", 1) if j == 15 else None)
        p.op("vector", "tensor_copy", srow[0:1, 0:1024], psA[0:1, 1024:2048],
             waits=[("ffT4", p.val("ffT4"))], drain=True)
        p.op("vector", "tensor_copy", srow[0:1, 1024:2048], psB[0:1, 1024:2048],
             inc=("ffV", 1))
        for half in range(2):
            cols = slice(1024 * half, 1024 * (half + 1))
            bc_row(psA[:, 0:1024], srow[0:1, cols],
                   waits=[("ffV", p.val("ffV"))] if half == 0 else [],
                   inc=("ffT4", 1))
            for c in range(NC):
                p.op("vector", "tensor_tensor", khat_ch(c)[:, cols],
                     khat_ch(c)[:, cols], psA[:, 0:1024], ALU.mult,
                     waits=[("ffT4", p.val("ffT4"))] if c == 0 else [],
                     drain=(c == 0))
            p.barrier(("tensor", "vector"))

        so_cols = scl[:, 8:16]
        wldf_t = [A2[:, 14336:15360], A2[:, 15360:16384]]
        for oc in range(NC):
            buf = wldf_t[oc % 2]
            ws = "dW" + str(oc % 2)
            w = [("twV", p.val("twV") - 1)] if oc >= 2 else []
            p.op("sync", "dma_start", out=buf[:],
                 in_=w_ffn[128 * oc:128 * (oc + 1), :], waits=w, inc=(ws, 16))
            p.op("vector", "reduce_sum", so_cols[:, oc:oc + 1], buf[:],
                 axis=AX.X, apply_absolute_value=True,
                 waits=[(ws, p.val(ws))], drain=True)
            p.op("vector", "tensor_single_scalar", so_cols[:, oc:oc + 1],
                 so_cols[:, oc:oc + 1], INVD, ALU.mult, drain=True)
            p.op("vector", "tensor_single_scalar", so_cols[:, oc:oc + 1],
                 so_cols[:, oc:oc + 1], 1e-5, ALU.max, drain=True,
                 inc=("twV", 1))
        if "socol" in taps:
            p.barrier()
            p.op("gpsimd", "dma_start", out=dbg["socol"][:, :], in_=so_cols,
                 waits=[("dDbg", p.val("dDbg"))], inc=("dDbg", 16))
            p.wait("gpsimd", "dDbg", p.val("dDbg"))
        so_row = A2[0:1, 12288:13312]
        for j in range(8):
            p.op("tensor", "transpose", psA[0:1, 1024 + 128 * j:1152 + 128 * j],
                 so_cols[:, j:j + 1], identf[:, :],
                 waits=[("twV", p.val("twV"))] if j == 0 else [],
                 inc=("twT", 1) if j == 7 else None)
        p.op("vector", "tensor_copy", so_row, psA[0:1, 1024:2048],
             waits=[("twT", p.val("twT"))], drain=True, inc=("twV", 1))
        hs_row = A2[0:1, 13312:14336]
        p.op("vector", "tensor_single_scalar", hs_row, so_row, 0.5, ALU.mult,
             drain=True, inc=("twV", 1))
        bc_row(psA[:, 0:1024], hs_row[0:1, :], waits=[("twV", p.val("twV"))],
               inc=("twT", 1))
        p.wait("vector", "twT", p.val("twT"))
        p.op("vector", "tensor_single_scalar", hs_row, hs_row, -1.0, ALU.mult,
             drain=True, inc=("twV", 1))
        bc_row(psA[:, 1024:2048], hs_row[0:1, :], waits=[("twV", p.val("twV"))],
               inc=("twT", 1))
        p.barrier(("tensor", "vector", "sync"))

        def ternT_ch(ic):
            return A2b[:, 16384 + 1024 * ic:16384 + 1024 * (ic + 1)]

        wldf_t = [A2[:, 14336:15360], A2[:, 15360:16384]]
        for ic in range(NC):
            buf = wldf_t[ic % 2]
            ws = "dW" + str(ic % 2)
            w = [("twV", p.val("twV") - 1)] if ic >= 2 else []
            p.op("sync", "dma_start", out=buf[:],
                 in_=w_ffnT[128 * ic:128 * (ic + 1), :], waits=w,
                 inc=(ws, 16))
            p.op("vector", "tensor_tensor", wqd[0][:], buf[:], psA[:, 0:1024],
                 ALU.is_gt, waits=[(ws, p.val(ws))], drain=True)
            p.op("vector", "tensor_tensor", wqd[1][:], buf[:], psA[:, 1024:2048],
                 ALU.is_lt, drain=True)
            p.op("vector", "tensor_tensor", ternT_ch(ic), wqd[0][:], wqd[1][:],
                 ALU.subtract, drain=True, inc=("twV", 1))
        p.barrier(("tensor", "vector", "sync"))
        dump_chunks("tern", ternT_ch, width=1024)

        for oc in range(NC):
            for qg in range(4):
                bank = psbank((oc * 4 + qg) % 8)
                for ic in range(NC):
                    w = []
                    if ic == 0 and oc * 4 + qg >= 8:
                        w.append(("ffE", p.val("ffE") - 7))
                    p.op("tensor", "matmul", bank,
                         ternT_ch(ic)[:, 128 * oc:128 * (oc + 1)],
                         khat_ch(ic)[:, 512 * qg:512 * (qg + 1)],
                         start=(ic == 0), stop=(ic == NC - 1), waits=w,
                         inc=("ffT5", 1) if ic == NC - 1 else None)
                p.op("vector", "scalar_tensor_tensor",
                     xc(oc)[:, 512 * qg:512 * (qg + 1)], bank,
                     so_cols[:, oc:oc + 1], xc(oc)[:, 512 * qg:512 * (qg + 1)],
                     op0=ALU.mult, op1=ALU.add,
                     waits=[("ffT5", p.val("ffT5"))],
                     inc=("ffE", 1))
        p.barrier()
        dump_chunks("x_ffn", xc)
        if stage <= 2:
            _emit_out_zero(p, out_t, A2)
            p.finalize(block)
            return nc, dbg

        def xdq_ch(c):
            return A2b[:, 2048 * c:2048 * (c + 1)]

        u_scr = A2[:, 8192:10240]
        q8_scr = A2f8[:, 4 * 10240:4 * 10240 + 2048]
        rrow_sb = A2[0:1, 12800:13824]
        srow_sb = A2[0:1, 13824:14848]
        for c in range(NC):
            transpose_reduce([a1f(c)[:, 128 * tt:128 * tt + 128]
                              for tt in range(NTT)],
                             lambda wv: scl[:, 8 * wv:8 * wv + 8],
                             "qdT", "qdV",
                             first_waits=[("qdV2", p.val("qdV2"))])
            p.op("vector", "tensor_single_scalar", scl[:, 0:16], scl[:, 0:16],
                 448e-12, ALU.max, drain=True)
            p.op("vector", "reciprocal", scl[:, 16:32], scl[:, 0:16],
                 drain=True)
            p.op("vector", "tensor_single_scalar", scl[:, 16:32],
                 scl[:, 16:32], R224, ALU.mult, drain=True)
            p.op("vector", "tensor_single_scalar", scl[:, 32:48],
                 scl[:, 0:16], INV224, ALU.mult, inc=("qdV", 1))
            for half in range(2):
                cols = slice(1024 * half, 1024 * (half + 1))
                tv0 = p.val("qdV")
                rv0 = p.val("qdV2")
                for j in range(8):
                    w = []
                    if j == 0:
                        w.append(("qdV", tv0))
                        if half == 1:
                            w.append(("qdV2", rv0))
                    p.op("tensor", "transpose",
                         psA[0:1, 1024 + 128 * j:1152 + 128 * j],
                         scl[:, 16 + 8 * half + j:17 + 8 * half + j],
                         identf[:, :], waits=w)
                    p.op("tensor", "transpose",
                         psB[0:1, 1024 + 128 * j:1152 + 128 * j],
                         scl[:, 32 + 8 * half + j:33 + 8 * half + j],
                         identf[:, :], inc=("qdT2", 1) if j == 7 else None)
                p.op("vector", "tensor_copy", rrow_sb, psA[0:1, 1024:2048],
                     waits=[("qdT2", p.val("qdT2"))], drain=True)
                p.op("vector", "tensor_copy", srow_sb, psB[0:1, 1024:2048],
                     inc=("qdV2", 1))
                bc_row(psA[:, 0:1024], rrow_sb[0:1, :],
                       waits=[("qdV2", p.val("qdV2"))], inc=("qdT3", 1))
                p.op("vector", "tensor_tensor", u_scr[:, cols], a1f(c)[:, cols],
                     psA[:, 0:1024], ALU.mult,
                     waits=[("qdT3", p.val("qdT3"))], drain=True,
                     inc=("qdV3", 1))
                p.op("scalar", "activation", q8_scr[:, cols], u_scr[:, cols],
                     AF.Copy, waits=[("qdV3", p.val("qdV3"))], drain=True,
                     inc=("qdS", 1))
                bc_row(psB[:, 0:1024], srow_sb[0:1, :],
                       waits=[("qdV2", p.val("qdV2"))], inc=("qdT3", 1))
                p.op("vector", "tensor_tensor", xdq_ch(c)[:, cols],
                     q8_scr[:, cols], psB[:, 0:1024], ALU.mult,
                     waits=[("qdT3", p.val("qdT3")), ("qdS", p.val("qdS"))],
                     inc=("qdV", 1))
        p.barrier()
        dump_chunks("xdq", xdq_ch)
        if stage <= 3:
            _emit_out_zero(p, out_t, A2)
            p.finalize(block)
            return nc, dbg

        VA0 = 16384

        def vaug_stat(tt, h):
            base = VA0 + (tt * 8 + h) * 65
            return A2b[:, base:base + 65]

        def vaug_w2(tt, h0):
            base = VA0 + (tt * 8 + h0) * 65
            return A2b[:, base:base + 130].rearrange(
                "p (h c) -> p h c", h=2)[:, :, 0:64]

        SCf = 15424
        o_u = A2[:, SCf:SCf + 512]
        o_q8 = A2f8[:, 4 * 15936:4 * 15936 + 512]
        vt_scr = A2b[:, 2 * 14400:2 * 14400 + 2048]
        dqt_scr = A2b[:, 2 * 16064:2 * 16064 + 512]

        def qh_ap(j, off, n):
            return A1b[:, j * 2048 + off:j * 2048 + off + n]

        def kh_ap(j, off, n):
            return A1b[:, 8192 + j * 2048 + off:8192 + j * 2048 + off + n]

        def odq_ch(c):
            return A1b[:, 16384 + 2048 * c:16384 + 2048 * (c + 1)]

        def emit_wqd(src3, oc):
            par = oc % 2
            buf = wldf[par]
            qb = wqd[par]
            ws = "dWq" + str(par)
            w = [("wqV", p.val("wqV") - 1)] if p.val("wqLd") >= 2 else []
            p.op("sync", "dma_start", out=buf[:], in_=src3[oc], waits=w,
                 inc=(ws, 16))
            p.bump("wqLd")
            b3 = buf[:].rearrange("p (ic c) -> p ic c", c=128)
            q3 = qb[:].rearrange("p (ic c) -> p ic c", c=128)
            e3 = w8[:].rearrange("p (ic c) -> p ic c", c=128)
            p.op("vector", "reduce_max", scl[:, 32:40], b3, axis=AX.X,
                 apply_absolute_value=True, waits=[(ws, p.val(ws))],
                 drain=True)
            p.op("vector", "tensor_single_scalar", scl[:, 32:40], scl[:, 32:40],
                 448e-12, ALU.max, drain=True)
            p.op("vector", "reciprocal", scl[:, 40:48], scl[:, 32:40],
                 drain=True)
            p.op("vector", "tensor_single_scalar", scl[:, 40:48], scl[:, 40:48],
                 R224, ALU.mult, drain=True)
            p.op("vector", "tensor_single_scalar", scl[:, 48:56], scl[:, 32:40],
                 INV224, ALU.mult, drain=True)
            rbc = ap3(scl, 40, [[64, 128], [1, 8], [0, 128]])
            sbc = ap3(scl, 48, [[64, 128], [1, 8], [0, 128]])
            p.op("vector", "tensor_tensor", b3, b3, rbc, ALU.mult, drain=True,
                 inc=("wqV0", 1))
            p.op("scalar", "activation", w8[:], buf[:], AF.Copy,
                 waits=[("wqV0", p.val("wqV0"))], drain=True, inc=("wqS", 1))
            p.op("vector", "tensor_tensor", q3, e3, sbc, ALU.mult,
                 waits=[("wqS", p.val("wqS"))], drain=True, inc=("wqV", 1))
            return p.val("wqV")

        def emit_l0_half(h2):
            order = []
            for kind in ("q", "k"):
                base_oc = (0 if kind == "q" else 8) + 4 * h2
                for j in range(4):
                    order.append((kind, base_oc + j, j))
            vvs = emit_wqd(wqkv_r, order[0][1])
            ev_hist = []
            for i, (kind, oc, j) in enumerate(order):
                for qg in range(4):
                    bank = psbank(qg)
                    for ic in range(NC):
                        w = []
                        if ic == 0:
                            w.append(("wqV", vvs))
                            if i >= 1:
                                w.append(("qkE", ev_hist[i - 1][qg]))
                        p.op("tensor", "matmul", bank,
                             wqd[oc % 2][:, 128 * ic:128 * (ic + 1)],
                             xdq_ch(ic)[:, 512 * qg:512 * (qg + 1)],
                             start=(ic == 0), stop=(ic == NC - 1),
                             waits=w,
                             inc=("qkT", 1) if ic == NC - 1 else None)
                if i + 1 < len(order):
                    vvs = emit_wqd(wqkv_r, order[i + 1][1])
                evs = []
                for qg in range(4):
                    dst = (qh_ap if kind == "q" else kh_ap)(
                        j, 512 * qg, 512)
                    w = [("qkT", p.val("qkT"))] if qg == 0 else []
                    if kind == "q":
                        p.op("vector", "tensor_scalar", dst, psbank(qg),
                             bqkv[:, oc:oc + 1], None, op0=ALU.add,
                             waits=w, inc=("qkE", 1))
                    else:
                        p.op("vector", "tensor_scalar", dst, psbank(qg),
                             bqkv[:, oc:oc + 1], 0.125, op0=ALU.add,
                             op1=ALU.mult,
                             waits=w, inc=("qkE", 1))
                    evs.append(p.val("qkE"))
                ev_hist.append(evs)
            p.barrier(("tensor", "vector", "scalar", "sync"))
            ones_ap = A2b[:, VA0 + 64:VA0 + 64 + 65 * 128].rearrange(
                "p (s c) -> p s c", c=65)[:, :, 0:1]
            p.op("vector", "memset", ones_ap, 1.0, drain=True)
            for j in range(4):
                oc = 16 + 4 * h2 + j
                vv = emit_wqd(wqkv_r, oc)
                for qg in range(4):
                    bank = psbank(qg)
                    for ic in range(NC):
                        w = [("wqV", vv)] if ic == 0 else []
                        if ic == 0 and qg == 0:
                            w.append(("vV", p.val("vV")))
                        if ic == 0 and qg == 0 and p.val("vE") >= 4:
                            w.append(("vE", p.val("vE") - 3))
                        p.op("tensor", "matmul", bank,
                             wqd[oc % 2][:, 128 * ic:128 * (ic + 1)],
                             xdq_ch(ic)[:, 512 * qg:512 * (qg + 1)],
                             start=(ic == 0), stop=(ic == NC - 1), waits=w,
                             inc=("qkT", 1) if ic == NC - 1 else None)
                for qg in range(4):
                    p.op("vector", "tensor_scalar",
                         vt_scr[:, 512 * qg:512 * (qg + 1)], psbank(qg),
                         bqkv[:, oc:oc + 1], None, op0=ALU.add,
                         waits=[("qkT", p.val("qkT"))] if qg == 0 else [],
                         inc=("vE", 1))
                transpose_tiles([(vt_scr[:, 128 * tt:128 * tt + 128],
                                  vaug_w2(tt, 2 * j)) for tt in range(NTT)],
                                identb[:, :], "vT", "vV",
                                first_waits=[("vE", p.val("vE"))])
            p.barrier(("tensor", "vector"))
            if h2 == 0 and "qT" in taps:
                p.barrier()
                for j in range(4):
                    p.op("gpsimd", "dma_start",
                         out=dbg["qT"][:, 2048 * j:2048 * (j + 1)],
                         in_=A1b[:, 2048 * j:2048 * (j + 1)],
                         waits=[("dDbg", p.val("dDbg"))], inc=("dDbg", 16))
                    p.op("gpsimd", "dma_start",
                         out=dbg["kT"][:, 2048 * j:2048 * (j + 1)],
                         in_=A1b[:, 8192 + 2048 * j:8192 + 2048 * (j + 1)],
                         waits=[("dDbg", p.val("dDbg"))], inc=("dDbg", 16))
                p.barrier()
            def sc_fn(s, idx, kc, ew):
                pr, qg = divmod(s, 4)
                par = idx % 2
                eps_ = psA[:, 1024 * par:1024 * par + 1024]
                p.op("tensor", "matmul", eps_[:, 0:512],
                     _kslice(A1b, pr, kc, 0), _qslice(A1b, pr, qg, 0),
                     start=True, stop=True,
                     waits=[("aS", ew)] if ew is not None else [],
                     tile_position=(0, 0))
                p.op("tensor", "matmul", eps_[:, 512:1024],
                     _kslice(A1b, pr, kc, 64), _qslice(A1b, pr, qg, 64),
                     start=True, stop=True, tile_position=(64, 0),
                     inc=("aT", 1))

            def eslot(idx):
                j = idx % 3
                return (ebuf[:, 1024 * j:1024 * (j + 1)] if j < 2
                        else A2b[:, 24704:25728])

            def ex_fn(s, idx, kc):
                par = idx % 2
                p.op("scalar", "activation", eslot(idx),
                     psA[:, 1024 * par:1024 * par + 1024],
                     AF.Exp, waits=[("aT", p.val("aT"))], inc=("aS", 1))

            def pv_fn(s, idx, kc, first, last, sv):
                pr, qg = divmod(s, 4)
                es = eslot(idx)
                base = 1024 * (s % 2)
                w0 = [sv]
                if first:
                    w0.append(("oV4", p.val("oV4")))
                p.op("tensor", "matmul", psB[0:65, base:base + 512],
                     vaug_stat(kc, 2 * pr), es[:, 0:512],
                     start=first, stop=last, waits=w0)
                p.op("tensor", "matmul", psB[0:65, base + 512:base + 1024],
                     vaug_stat(kc, 2 * pr + 1), es[:, 512:1024],
                     start=first, stop=last,
                     inc=("aT2", 1) if last else None)

            def parts_fn(s):
                pr, qg = divmod(s, 4)
                return _onorm_qd_parts(p, 4 * h2 + pr, qg, s % 2, psB, A2,
                                       A2f8, identf, identb, scl, odq_ch,
                                       o_u, o_q8, dqt_scr, ap3)

            _attn_pipeline(p, 16, lambda s: list(range(NTT)), sc_fn, ex_fn,
                           pv_fn, parts_fn)
            p.barrier()

        emit_l0_half(0)
        emit_l0_half(1)
        dump_chunks("odq", odq_ch)
        if "vaug" in taps:
            p.op("gpsimd", "dma_start", out=dbg["vaug"][:, :],
                 in_=A2b[:, VA0:VA0 + 8320],
                 waits=[("dDbg", p.val("dDbg"))], inc=("dDbg", 16))
            p.barrier()
        if stage <= 4:
            _emit_out_zero(p, out_t, A2)
            p.finalize(block)
            return nc, dbg

        vvs = emit_wqd(wout_r, 0)
        ev_hist = []
        for oc in range(NC):
            for qg in range(4):
                bank = psbank(qg)
                for ic in range(NC):
                    w = []
                    if ic == 0:
                        w.append(("wqV", vvs))
                        if oc >= 1:
                            w.append(("opE", ev_hist[oc - 1][qg]))
                    p.op("tensor", "matmul", bank,
                         wqd[oc % 2][:, 128 * ic:128 * (ic + 1)],
                         odq_ch(ic)[:, 512 * qg:512 * (qg + 1)],
                         start=(ic == 0), stop=(ic == NC - 1), waits=w,
                         inc=("qkT", 1) if ic == NC - 1 else None)
            if oc + 1 < NC:
                vvs = emit_wqd(wout_r, oc + 1)
            evs = []
            for qg in range(4):
                xl = xc(oc)[:, 512 * qg:512 * (qg + 1)]
                p.op("vector", "scalar_tensor_tensor", xl, psbank(qg),
                     bout[:, oc:oc + 1], xl, op0=ALU.add, op1=ALU.add,
                     waits=[("qkT", p.val("qkT"))] if qg == 0 else [],
                     inc=("opE", 1))
                evs.append(p.val("opE"))
            ev_hist.append(evs)
        p.barrier()
        dump_chunks("x_l0", xc)
        if stage <= 5:
            _emit_out_zero(p, out_t, A2)
            p.finalize(block)
            return nc, dbg

        emit_layernorm(a1f, rows_off=0, sq_off=8192)
        p.barrier()
        dump_chunks("xn2", a1f)

        xnb_ch = lambda c: A2b[:, 2048 * c:2048 * (c + 1)]
        for c in range(NC):
            p.op("vector", "tensor_copy", xnb_ch(c), a1f(c), drain=(c == 0),
                 inc=("gV", 1) if c == NC - 1 else None)
        xlo_scr = wldf[0][:].bitcast(BF16)
        l_row = A2[0:1, 10240:12288]
        for ic in range(NC):
            p.op("vector", "tensor_tensor", xlo_scr[:, 0:2048],
                 a1f(ic), xnb_ch(ic), ALU.subtract, drain=(ic == 0),
                 waits=[("gT3", 4 * ic)] if ic > 0 else [],
                 inc=("gV", 1))
            gv = p.val("gV")
            for qg in range(4):
                for ps_i, (stat, mov) in enumerate(
                        ((wgh[:, ic:ic + 1], xnb_ch(ic)),
                         (wgl[:, ic:ic + 1], xnb_ch(ic)),
                         (wgh[:, ic:ic + 1], xlo_scr[:, 0:2048]))):
                    p.op("tensor", "matmul", psA[0:1, 512 * qg:512 * (qg + 1)],
                         stat, mov[:, 512 * qg:512 * (qg + 1)],
                         start=(ic == 0 and ps_i == 0),
                         stop=(ic == NC - 1 and ps_i == 2),
                         waits=[("gV", gv)] if ps_i == 0 else [],
                         inc=("gT3", 1) if ps_i == 2 else None)
        p.wait("vector", "gT3", p.val("gT3"))
        p.op("vector", "tensor_copy", l_row, psA[0:1, 0:2048], drain=True,
             inc=("gV", 1))
        if "lrow" in taps:
            p.barrier()
            p.op("gpsimd", "dma_start", out=dbg["lrow"][:, :], in_=l_row,
                 waits=[("dDbg", p.val("dDbg"))], inc=("dDbg", 16))
            p.wait("gpsimd", "dDbg", p.val("dDbg"))
            p.barrier()
        for tt in range(NTT):
            p.op("tensor", "transpose", psA[:, 1024 + tt:1025 + tt],
                 A2[0:1, 10240 + 128 * tt:10368 + 128 * tt], identf[0:1, 0:1],
                 waits=[("gV", p.val("gV"))] if tt == 0 else [],
                 inc=("gT", 1) if tt == NTT - 1 else None)
        p.op("vector", "tensor_copy", scl[:, 0:16], psA[:, 1024:1040],
             waits=[("gT", p.val("gT"))], drain=True, inc=("gV", 1))
        lbc = A2[:, 8192:10240]
        for half in range(2):
            bc_row(psA[:, 0:1024], l_row[0:1, 1024 * half:1024 * (half + 1)],
                   waits=[("gV", p.val("gV"))],
                   inc=("gT", 1))
            p.op("vector", "tensor_copy", lbc[:, 1024 * half:1024 * (half + 1)],
                 psA[:, 0:1024], waits=[("gT", p.val("gT"))], drain=True,
                 inc=("gV", 1))
        p.barrier(("tensor", "vector"))
        lo_c = scl[:, 48:49]
        hi_c = scl[:, 49:50]
        st_c = scl[:, 50:51]
        th_c = scl[:, 51:52]
        cnt_c = scl[:, 52:53]
        ok_c = scl[:, 53:54]
        p.op("vector", "memset", lo_c, -512.0, drain=True)
        p.op("vector", "memset", hi_c, 512.0, drain=True, inc=("gV", 1))
        for it in range(10):
            p.op("vector", "tensor_tensor", st_c, hi_c, lo_c, ALU.subtract,
                 waits=[("gT2", p.val("gT2"))], drain=True)
            p.op("vector", "tensor_single_scalar", st_c, st_c,
                 float(np.float32(1.0 / 128.0)), ALU.mult, drain=True)
            p.op("vector", "scalar_tensor_tensor", th_c, iotac[:, 0:1], st_c,
                 lo_c, op0=ALU.mult, op1=ALU.add, drain=True)
            p.op("vector", "tensor_scalar", psA[:, 0:2048], lbc, th_c, None,
                 op0=ALU.is_gt, op1=ALU.add, accum_out=cnt_c, drain=True)
            p.op("vector", "tensor_single_scalar", ok_c, cnt_c, 1024.0,
                 ALU.is_ge, drain=True, inc=("gV2", 1))
            p.op("tensor", "matmul", psB[0:1, 0:1], onescol[:, 0:1],
                 ok_c, start=True, stop=True,
                 waits=[("gV2", p.val("gV2"))], inc=("gT2", 1))
            p.op("vector", "tensor_copy", scl[0:1, 57:58], psB[0:1, 0:1],
                 waits=[("gT2", p.val("gT2"))], drain=True, inc=("gV2", 1))
            p.op("tensor", "matmul", psB[:, 1:2], onesrow[0:1, :],
                 scl[0:1, 57:58], start=True, stop=True,
                 waits=[("gV2", p.val("gV2"))], inc=("gT2", 1))
            p.op("vector", "scalar_tensor_tensor", th_c, psB[:, 1:2], -1.0,
                 st_c, op0=ALU.add, op1=ALU.mult,
                 waits=[("gT2", p.val("gT2"))], drain=True)
            p.op("vector", "tensor_add", lo_c, th_c, lo_c, drain=True)
            p.op("vector", "tensor_add", hi_c, lo_c, st_c, drain=True,
                 inc=("gV3", 1))
        p.barrier(("tensor", "vector"))
        if "locol" in taps:
            p.barrier()
            p.op("gpsimd", "dma_start", out=dbg["locol"][:, :], in_=lo_c,
                 waits=[("dDbg", p.val("dDbg"))], inc=("dDbg", 16))
            p.wait("gpsimd", "dDbg", p.val("dDbg"))
            p.barrier()
        sel_f = scl[:, 16:32]
        p.op("vector", "tensor_scalar", sel_f, scl[:, 0:16], lo_c, None,
             op0=ALU.is_gt, drain=True)
        sel_b = sclb[:, 112:128]
        p.op("vector", "tensor_copy", sel_b, sel_f, drain=True, inc=("gV", 1))
        p.op("tensor", "matmul", psB[:, 0:16], tri_b[:, :], sel_b,
             start=True, stop=True, waits=[("gV", p.val("gV"))],
             inc=("gT", 1))
        p.op("tensor", "matmul", psB[0:1, 16:32], onescol[:, 0:1], sel_f,
             start=True, stop=True, waits=[("gV", p.val("gV"))],
             inc=("gT", 1))
        p.wait("vector", "gT", p.val("gT"))
        p.op("vector", "tensor_copy", A2[0:1, 14400:14416], psB[0:1, 16:32],
             drain=True, inc=("gV", 1))
        p.barrier(("vector", "sync"))
        p.op("sync", "dma_start", out=scl[0:16, 56:57],
             in_=A2[0:1, 14400:14416], inc=("dF", 16))
        p.wait("tensor", "dF", p.val("dF"))
        p.op("tensor", "matmul", psB[0:16, 16:17], tri_f[0:16, 0:16],
             scl[0:16, 56:57], start=True, stop=True, inc=("gT", 1))
        p.wait("vector", "gT", p.val("gT"))
        p.op("vector", "tensor_copy", scl[0:16, 57:58], psB[0:16, 16:17],
             drain=True, inc=("gV", 1))
        p.op("tensor", "transpose", psB[0:1, 17:33], scl[0:16, 57:58],
             tri_f[0:16, 0:16] if False else identf[0:16, 0:16],
             waits=[("gV", p.val("gV"))], inc=("gT", 1))
        p.wait("vector", "gT", p.val("gT"))
        p.op("vector", "tensor_copy", A2[0:1, 14416:14432], psB[0:1, 17:33],
             drain=True, inc=("gV", 1))
        bc_row(psB[:, 32:48], A2[0:1, 14416:14432],
               waits=[("gV", p.val("gV"))], inc=("gT", 1))
        p.wait("vector", "gT", p.val("gT"))
        p.op("vector", "tensor_copy", scl[:, 16:32], psB[:, 0:16], drain=True)
        p.op("vector", "tensor_tensor", scl[:, 16:32], scl[:, 16:32],
             psB[:, 32:48], ALU.add, drain=True)
        p.op("vector", "tensor_single_scalar", scl[:, 32:48], scl[:, 16:32],
             4096.0, ALU.add, drain=True)
        p.op("vector", "tensor_scalar", sel_f, scl[:, 0:16], lo_c, None,
             op0=ALU.is_gt, drain=True)
        p.op("vector", "scalar_tensor_tensor", scl[:, 32:48], sel_f, -4096.0,
             scl[:, 32:48], op0=ALU.mult, op1=ALU.add, drain=True,
             inc=("gV", 1))
        if "rankp" in taps:
            p.barrier()
            p.op("gpsimd", "dma_start", out=dbg["rankp"][:, :],
                 in_=scl[:, 32:48], waits=[("dDbg", p.val("dDbg"))],
                 inc=("dDbg", 16))
            p.wait("gpsimd", "dDbg", p.val("dDbg"))
            p.barrier()
        lcol_b = sclb[:, 96:112]
        p.op("vector", "tensor_copy", lcol_b, scl[:, 0:16], drain=True,
             inc=("gV", 1))
        p.barrier()

        pairs2 = []
        for tt in range(NTT):
            for fc in range(NC):
                src2 = xnb_ch(fc)[:, 128 * tt:128 * tt + 128]
                dst2 = A2b[:, 16384 + 1024 * tt + 128 * fc:
                           16384 + 1024 * tt + 128 * fc + 128]
                pairs2.append((src2, dst2))
        psBb2 = psB[:].bitcast(BF16)
        transpose_tiles(pairs2, identb[:, :], "tT", "tV")
        p.barrier()

        def attn_in_ch(fc):
            return A1b[:, 1024 * fc:1024 * (fc + 1)]

        bi = 0
        for sg in range(2):
            for tt in range(NTT):
                eb = ebuf[:, 512 * (bi % 2):512 * (bi % 2) + 512]
                w = [("gaT", p.val("gaT") - 1)] if bi >= 2 else []
                p.op("vector", "tensor_scalar", eb,
                     iotar[:, 512 * sg:512 * (sg + 1)], scl[:, 32 + tt:33 + tt],
                     None, op0=ALU.is_equal,
                     waits=w, drain=(bi == 0), inc=("gaV", 1))
                gav = p.val("gaV")
                for fc in range(NC):
                    p.op("tensor", "matmul", psbank(fc),
                         A2b[:, 16384 + 1024 * tt + 128 * fc:
                             16384 + 1024 * tt + 128 * fc + 128],
                         eb, start=(tt == 0), stop=(tt == NTT - 1),
                         waits=[("gaV", gav)] if fc == 0 else [],
                         inc=("gaT", 1) if fc == NC - 1 else None)
                bi += 1
            gtv = p.val("gaT")
            for hh in range(2):
                dst = bass.AP(A1b.tensor, 4096 * hh + 512 * sg,
                              [[32768, 128], [1024, 4], [1, 512]])
                p.op("vector", "tensor_copy", dst,
                     (psA if hh == 0 else psB)[:, 0:2048].rearrange(
                         "p (f t) -> p f t", f=4),
                     waits=[("gaT", gtv)] if hh == 0 else [],
                     inc=("gaV", 1) if hh == 1 else None)
        for sg in range(2):
            for tt in range(NTT):
                eb = ebuf[:, 512 * (bi % 2):512 * (bi % 2) + 512]
                w = [("gaT", p.val("gaT") - 1)] if bi >= 2 else []
                p.op("vector", "tensor_scalar", eb,
                     iotar[:, 512 * sg:512 * (sg + 1)], scl[:, 32 + tt:33 + tt],
                     None, op0=ALU.is_equal,
                     waits=w, inc=("gaV", 1))
                gav = p.val("gaV")
                p.op("tensor", "matmul", psA[0:1, 1024:1536],
                     lcol_b[:, tt:tt + 1], eb,
                     start=(tt == 0), stop=(tt == NTT - 1),
                     waits=[("gaV", gav)], inc=("gaT", 1))
                bi += 1
            p.op("vector", "tensor_copy",
                 A2[0:1, 14400 + 512 * sg:14400 + 512 * (sg + 1)],
                 psA[0:1, 1024:1536], waits=[("gaT", p.val("gaT"))],
                 drain=True, inc=("gaV", 1))
        p.barrier(("tensor", "vector", "scalar"))
        if "attn_in" in taps:
            dump_chunks("attn_in", attn_in_ch, width=1024)
        if "lsel" in taps:
            p.barrier()
            p.op("gpsimd", "dma_start", out=dbg["lsel"][:, :],
                 in_=A2[0:1, 14400:15424],
                 waits=[("dDbg", p.val("dDbg"))], inc=("dDbg", 16))
            p.wait("gpsimd", "dDbg", p.val("dDbg"))
            p.barrier()
        lsel_row = A2[0:1, 14400:15424]
        p.op("scalar", "activation", lsel_row, lsel_row, AF.Exp, scale=-1.0,
             drain=True, inc=("gS", 1))
        p.op("vector", "tensor_single_scalar", lsel_row, lsel_row, 1.0,
             ALU.add, waits=[("gS", p.val("gS"))], drain=True)
        p.op("vector", "reciprocal", lsel_row, lsel_row, drain=True,
             inc=("gV", 1))
        for cc in range(8):
            p.op("tensor", "transpose", psA[:, 1024 + cc:1025 + cc],
                 A2[0:1, 14400 + 128 * cc:14528 + 128 * cc], identf[0:1, 0:1],
                 waits=[("gV", p.val("gV"))] if cc == 0 else [],
                 inc=("gT", 1) if cc == 7 else None)
        p.op("vector", "tensor_copy", scl[:, 48:56], psA[:, 1024:1032],
             waits=[("gT", p.val("gT"))], drain=True, inc=("gV", 1))
        if "gcols" in taps:
            p.barrier()
            p.op("gpsimd", "dma_start", out=dbg["gcols"][:, :],
                 in_=scl[:, 48:56], waits=[("dDbg", p.val("dDbg"))],
                 inc=("dDbg", 16))
            p.wait("gpsimd", "dDbg", p.val("dDbg"))
            p.barrier()

        def qf_ap(j, off, n):
            return A1b[:, 8192 + j * 1024 + off:8192 + j * 1024 + off + n]

        def kf_ap(j, off, n):
            return A1b[:, 16384 + j * 1024 + off:16384 + j * 1024 + off + n]

        def of_ch(fc):
            return A1b[:, 24576 + 1024 * fc:24576 + 1024 * (fc + 1)]

        VAF = 16384

        def vaugf_stat(sc, h):
            base = VAF + (sc * 16 + h) * 65
            return A2b[:, base:base + 65]

        def vaugf_w2(sc, h0):
            base = VAF + (sc * 16 + h0) * 65
            return A2b[:, base:base + 130].rearrange(
                "p (h c) -> p h c", h=2)[:, :, 0:64]

        ones_ap2 = A2b[:, VAF + 64:VAF + 64 + 65 * 128].rearrange(
            "p (s c) -> p s c", c=65)[:, :, 0:1]
        p.op("vector", "memset", ones_ap2, 1.0, drain=True)
        vt2 = ebuf[:, 1024:2048]
        fevh = []
        for kind, wsrc in (("q", fw_r["Wq"]), ("k", fw_r["Wk"]),
                           ("v", fw_r["Wv"])):
            for oc in range(NC):
                par = oc % 2
                ws = "dL" + str(par)
                p.op("gpsimd", "dma_start", out=wqd[par][:], in_=wsrc[oc],
                     waits=[("fpT", max(0, p.val("fpT") - 2))],
                     inc=(ws, 16))
                wvv = p.val(ws)
                qgs = 2
                for qg in range(qgs):
                    bank = psbank(qg)
                    for ic in range(NC):
                        w = [(ws, wvv)] if ic == 0 else []
                        if ic == 0 and fevh:
                            w.append(("fpV", fevh[-1][qg]))
                        p.op("tensor", "matmul", bank,
                             wqd[par][:, 128 * ic:128 * (ic + 1)],
                             attn_in_ch(ic)[:, 512 * qg:512 * (qg + 1)],
                             start=(ic == 0), stop=(ic == NC - 1), waits=w,
                             inc=("fpT", 1) if ic == NC - 1 else None)
                evs = []
                if kind == "q":
                    for qg in range(qgs):
                        p.op("vector", "tensor_copy",
                             qf_ap(oc, 512 * qg, 512),
                             psbank(qg),
                             waits=[("fpT", p.val("fpT"))] if qg == 0 else [],
                             inc=("fpV", 1))
                        evs.append(p.val("fpV"))
                elif kind == "k":
                    for qg in range(qgs):
                        p.op("vector", "tensor_single_scalar",
                             kf_ap(oc, 512 * qg, 512),
                             psbank(qg), 0.125, ALU.mult,
                             waits=[("fpT", p.val("fpT"))] if qg == 0 else [],
                             inc=("fpV", 1))
                        evs.append(p.val("fpV"))
                else:
                    for qg in range(qgs):
                        p.op("vector", "tensor_copy",
                             vt2[:, 512 * qg:512 * (qg + 1)], psbank(qg),
                             waits=[("fpT", p.val("fpT"))] if qg == 0 else [],
                             inc=("fpV", 1))
                        evs.append(p.val("fpV"))
                    transpose_tiles([(vt2[:, 128 * st2:128 * st2 + 128],
                                      vaugf_w2(st2, 2 * (oc % 8)))
                                     for st2 in range(8)],
                                    identb[:, :], "fvT", "fvV",
                                    first_waits=[("fpV", p.val("fpV"))])
                fevh.append(evs)
        p.barrier()

        def fsc_fn(s, idx, kc, ew):
            pr, qg = divmod(s, 2)
            par = idx % 2
            eps_ = psA[:, 1024 * par:1024 * par + 1024]
            kt = A1b[:, 16384 + pr * 1024 + 128 * kc:
                     16384 + pr * 1024 + 128 * kc + 128]
            qt = A1b[:, 8192 + pr * 1024 + 512 * qg:
                     8192 + pr * 1024 + 512 * qg + 512]
            p.op("tensor", "matmul", eps_[:, 0:512], kt[0:64, :], qt[0:64, :],
                 start=True, stop=True,
                 waits=[("aS", ew)] if ew is not None else [],
                 tile_position=(0, 0))
            p.op("tensor", "matmul", eps_[:, 512:1024], kt[64:128, :],
                 qt[64:128, :], start=True, stop=True, tile_position=(64, 0),
                 inc=("aT", 1))

        def feslot(idx):
            j = idx % 3
            return (ebuf[:, 1024 * j:1024 * (j + 1)] if j < 2
                    else A2b[:, 24704:25728])

        def fex_fn(s, idx, kc):
            par = idx % 2
            p.op("scalar", "activation", feslot(idx),
                 psA[:, 1024 * par:1024 * par + 1024], AF.Exp,
                 waits=[("aT", p.val("aT"))], inc=("aS", 1))

        def fpv_fn(s, idx, kc, first, last, sv):
            pr, qg = divmod(s, 2)
            es = feslot(idx)
            base = 1024 * (s % 2)
            w0 = [sv]
            if first:
                w0.append(("oV4", p.val("oV4")))
            p.op("tensor", "matmul", psB[0:65, base:base + 512],
                 vaugf_stat(kc, 2 * pr), es[:, 0:512],
                 start=first, stop=last, waits=w0)
            p.op("tensor", "matmul", psB[0:65, base + 512:base + 1024],
                 vaugf_stat(kc, 2 * pr + 1), es[:, 512:1024],
                 start=first, stop=last,
                 inc=("aT2", 1) if last else None)

        def fparts_fn(s):
            pr, qg = divmod(s, 2)
            return _onorm_plain_parts(p, pr, qg, s % 2, psB, A2, A2b, identf,
                                      identb, scl, of_ch, ap3)

        _attn_pipeline(p, 16, lambda s: list(range(8)), fsc_fn, fex_fn,
                       fpv_fn, fparts_fn)
        p.barrier()

        def y2_ch(sc):
            return A1b[:, 1024 * sc:1024 * (sc + 1)]

        for og in range(2):
            for fc in range(NC):
                par = fc % 2
                ws = "dL" + str(par)
                p.op("gpsimd", "dma_start", out=wqd[par][:],
                     in_=fWo_nat[128 * fc:128 * (fc + 1), :],
                     waits=[("y2T", max(0, p.val("y2T") - 1))], inc=(ws, 16))
                wvv = p.val(ws)
                for sc in range(NC):
                    w = [(ws, wvv)] if sc == 0 else []
                    if fc == 0 and sc == 0:
                        w.append(("y2V", p.val("y2V")))
                    p.op("tensor", "matmul", psbank(sc),
                         of_ch(fc)[:, 128 * sc:128 * (sc + 1)],
                         wqd[par][:, 512 * og:512 * (og + 1)],
                         start=(fc == 0), stop=(fc == NC - 1), waits=w,
                         inc=("y2T", 1) if sc == NC - 1 else None)
            for sc in range(NC):
                p.op("vector", "tensor_scalar", y2_ch(sc)[:, 512 * og:512 * (og + 1)],
                     psbank(sc), scl[:, 48 + sc:49 + sc], None, op0=ALU.mult,
                     waits=[("y2T", p.val("y2T"))] if sc == 0 else [],
                     drain=(sc == 0), inc=("y2V", 1))
        p.barrier()
        if "y2g" in taps:
            dump_chunks("y2g", y2_ch, width=1024)

        for j in range(16):
            pst = psA if j < 8 else psB
            p.op("tensor", "transpose",
                 pst[0:1, 1024 + 128 * (j % 8):1152 + 128 * (j % 8)],
                 scl[:, 32 + j:33 + j], identf[:, :],
                 waits=[("scV", p.val("scV"))] if j == 0 else [],
                 inc=("scT", 1) if j == 15 else None)
        p.op("vector", "tensor_copy", A2[0:1, 12352:13376],
             psA[0:1, 1024:2048],
             waits=[("scT", p.val("scT"))], drain=True)
        p.op("vector", "tensor_copy", A2[0:1, 13376:14400],
             psB[0:1, 1024:2048], inc=("scV", 1))
        rankbc = A2[:, 12352:14400]
        for half in range(2):
            bc_row(psA[:, 0:1024],
                   A2[0:1, 12352 + 1024 * half:12352 + 1024 * (half + 1)],
                   waits=[("scV", p.val("scV"))], inc=("scT", 1))
            p.op("vector", "tensor_copy",
                 rankbc[:, 1024 * half:1024 * (half + 1)], psA[:, 0:1024],
                 waits=[("scT", p.val("scT"))], drain=True, inc=("scV", 1))
        p.barrier(("tensor", "vector"))
        for sc in range(NC):
            p.op("vector", "tensor_single_scalar", scl[:, 16 + sc:17 + sc],
                 iotac[:, 0:1], float(128 * sc), ALU.add, drain=(sc == 0))
        bi2 = 0
        for tg in range(4):
            for sc in range(NC):
                eb = ebuf[:, 512 * (bi2 % 2):512 * (bi2 % 2) + 512]
                w = [("scT", p.val("scT") - 1)] if bi2 >= 2 else []
                p.op("vector", "tensor_scalar", eb,
                     rankbc[:, 512 * tg:512 * (tg + 1)],
                     scl[:, 16 + sc:17 + sc], None,
                     op0=ALU.is_equal, waits=w, drain=(bi2 == 0),
                     inc=("scV", 1))
                scv = p.val("scV")
                for fc in range(NC):
                    p.op("tensor", "matmul", psbank(fc),
                         y2_ch(sc)[:, 128 * fc:128 * (fc + 1)],
                         eb, start=(sc == 0), stop=(sc == NC - 1),
                         waits=[("scV", scv)] if fc == 0 else [],
                         inc=("scT", 1) if fc == NC - 1 else None)
                bi2 += 1
            stv = p.val("scT")
            for hh in range(2):
                dst = bass.AP(AXr[0:128, 0:1].tensor, 8192 * hh + 512 * tg,
                              [[16384, 128], [2048, 4], [1, 512]])
                p.op("vector", "tensor_tensor", dst, dst,
                     (psA if hh == 0 else psB)[:, 0:2048].rearrange(
                         "p (f t) -> p f t", f=4), ALU.add,
                     waits=[("scT", stv)] if hh == 0 else [],
                     inc=("scV", 1) if hh == 1 else None)
        p.barrier()
        dump_chunks("x_ffn2", xc)
        if stage <= 7:
            _emit_out_zero(p, out_t, A2)
            p.finalize(block)
            return nc, dbg

        lmask_sb = A2b[:, 24704:28800]
        p.op("gpsimd", "dma_start", out=lmask_sb, in_=c_lmasks[:, :],
             inc=("dInG", 16))
        p.wait("vector", "dInG", p.val("dInG"))

        def q2_ap(j, off, n):
            return A1b[:, j * 2048 + off:j * 2048 + off + n]

        def k2_ap(j, off, n):
            return A1b[:, 8192 + j * 2048 + off:8192 + j * 2048 + off + n]

        def o2_ch(c):
            return A1b[:, 16384 + 2048 * c:16384 + 2048 * (c + 1)]

        VL = 16384

        def vaug2_stat(tt, h):
            base = VL + (tt * 8 + h) * 65
            return A2b[:, base:base + 65]

        def vaug2_w2(tt, h0):
            base = VL + (tt * 8 + h0) * 65
            return A2b[:, base:base + 130].rearrange(
                "p (h c) -> p h c", h=2)[:, :, 0:64]

        vt3 = A2b[:, 28800:30848]

        def emit_l1_half(h2):
            ones_ap3 = A2b[:, VL + 64:VL + 64 + 65 * 128].rearrange(
                "p (s c) -> p s c", c=65)[:, :, 0:1]
            p.op("vector", "memset", ones_ap3, 1.0, drain=True,
                 inc=("l1V", 1))
            evh = []
            for kind, wsrc, bias in (("q", l1w_r["wq"], bcol["q"]),
                                     ("k", l1w_r["wk"], bcol["k"]),
                                     ("v", l1w_r["wv"], bcol["v"])):
                for j in range(4):
                    oc = 4 * h2 + j
                    par = oc % 2
                    ws = "dL" + str(par)
                    p.op("gpsimd", "dma_start", out=wqd[par][:], in_=wsrc[oc],
                         waits=[("l1T", max(0, p.val("l1T") - 4))],
                         inc=(ws, 16))
                    wvv = p.val(ws)
                    for qg in range(4):
                        bank = psbank(qg)
                        for ic in range(NC):
                            w = [(ws, wvv)] if ic == 0 else []
                            if ic == 0 and evh:
                                w.append(("l1E", evh[-1][qg]))
                            if ic == 0 and qg == 0:
                                w.append(("l1vV", p.val("l1vV")))
                            p.op("tensor", "matmul", bank,
                                 wqd[par][:, 128 * ic:128 * (ic + 1)],
                                 xnb_ch(ic)[:, 512 * qg:512 * (qg + 1)],
                                 start=(ic == 0), stop=(ic == NC - 1), waits=w,
                                 inc=("l1T", 1) if ic == NC - 1 else None)
                    evs = []
                    if kind == "q":
                        for qg in range(4):
                            p.op("vector", "tensor_scalar",
                                 q2_ap(j, 512 * qg, 512), psbank(qg),
                                 bias[:, oc:oc + 1], None, op0=ALU.add,
                                 waits=[("l1T", p.val("l1T"))] if qg == 0
                                 else [],
                                 inc=("l1E", 1))
                            evs.append(p.val("l1E"))
                    elif kind == "k":
                        for qg in range(4):
                            p.op("vector", "tensor_scalar",
                                 k2_ap(j, 512 * qg, 512), psbank(qg),
                                 bias[:, oc:oc + 1], 0.125, op0=ALU.add,
                                 op1=ALU.mult,
                                 waits=[("l1T", p.val("l1T"))] if qg == 0
                                 else [],
                                 inc=("l1E", 1))
                            evs.append(p.val("l1E"))
                    else:
                        for qg in range(4):
                            p.op("vector", "tensor_scalar",
                                 vt3[:, 512 * qg:512 * (qg + 1)], psbank(qg),
                                 bias[:, oc:oc + 1], None, op0=ALU.add,
                                 waits=[("l1T", p.val("l1T"))] if qg == 0
                                 else [],
                                 inc=("l1E", 1))
                            evs.append(p.val("l1E"))
                        transpose_tiles([(vt3[:, 128 * tt:128 * tt + 128],
                                          vaug2_w2(tt, 2 * j))
                                         for tt in range(NTT)],
                                        identb[:, :], "l1vT", "l1vV",
                                        first_waits=[("l1E",
                                                      p.val("l1E"))])
                    evh.append(evs)
            p.barrier(("tensor", "vector"))
            def lkcs_fn(s):
                qg = s % 4
                return [kc for kc in range(4 * qg - 2, 4 * qg + 6)
                        if 0 <= kc < 16]

            def lsc_fn(s, idx, kc, ew):
                pr, qg = divmod(s, 4)
                par = idx % 2
                eps_ = psA[:, 1024 * par:1024 * par + 1024]
                kt = A1b[:, 8192 + pr * 2048 + 128 * kc:
                         8192 + pr * 2048 + 128 * kc + 128]
                qt = A1b[:, pr * 2048 + 512 * qg:pr * 2048 + 512 * qg + 512]
                p.op("tensor", "matmul", eps_[:, 0:512], kt[0:64, :],
                     qt[0:64, :], start=True, stop=True,
                     waits=[("aS", ew)] if ew is not None else [],
                     tile_position=(0, 0))
                p.op("tensor", "matmul", eps_[:, 512:1024], kt[64:128, :],
                     qt[64:128, :], start=True, stop=True,
                     tile_position=(64, 0), inc=("aT", 1))

            def leslot(idx):
                j = idx % 3
                return (ebuf[:, 1024 * j:1024 * (j + 1)] if j < 2
                        else A2b[:, 31360:32384])

            def lex_fn(s, idx, kc):
                par = idx % 2
                p.op("scalar", "activation", leslot(idx),
                     psA[:, 1024 * par:1024 * par + 1024], AF.Exp,
                     waits=[("aT", p.val("aT"))], inc=("aS", 1))

            def lmk_fn(s, idx, kc):
                pr, qg = divmod(s, 4)
                es = leslot(idx)
                mi = kc - 4 * qg + 2
                lm = lmask_sb[:, 512 * mi:512 * (mi + 1)]
                eng = "gpsimd" if idx % 2 == 0 else "vector"
                sem = "aM" if idx % 2 == 0 else "aMv"
                p.op(eng, "tensor_tensor", es[:, 0:512], es[:, 0:512],
                     lm, ALU.mult, waits=[("aS", p.val("aS"))])
                p.op(eng, "tensor_tensor", es[:, 512:1024],
                     es[:, 512:1024], lm, ALU.mult, inc=(sem, 1))
                return (sem, p.val(sem))

            def lpv_fn(s, idx, kc, first, last, sv):
                pr, qg = divmod(s, 4)
                es = leslot(idx)
                base = 1024 * (s % 2)
                w0 = [sv]
                if first:
                    w0.append(("oV4", p.val("oV4")))
                p.op("tensor", "matmul", psB[0:65, base:base + 512],
                     vaug2_stat(kc, 2 * pr), es[:, 0:512],
                     start=first, stop=last, waits=w0)
                p.op("tensor", "matmul", psB[0:65, base + 512:base + 1024],
                     vaug2_stat(kc, 2 * pr + 1), es[:, 512:1024],
                     start=first, stop=last,
                     inc=("aT2", 1) if last else None)

            def lparts_fn(s):
                pr, qg = divmod(s, 4)
                return _onorm_plain_parts(p, 4 * h2 + pr, qg, s % 2, psB, A2,
                                          A2b, identf, identb, scl, o2_ch,
                                          ap3)

            _attn_pipeline(p, 16, lkcs_fn, lsc_fn, lex_fn, lpv_fn, lparts_fn,
                           emit_mask=lmk_fn)
            p.barrier()

        emit_l1_half(0)
        emit_l1_half(1)
        dump_chunks("o2", o2_ch, n=8, width=2048)

        oevh = []
        for oc in range(NC):
            par = oc % 2
            ws = "dL" + str(par)
            p.op("gpsimd", "dma_start", out=wqd[par][:], in_=l1w_r["wo"][oc],
                 waits=[("opT", max(0, p.val("opT") - 4))], inc=(ws, 16))
            wvv = p.val(ws)
            for qg in range(4):
                bank = psbank(qg)
                for ic in range(NC):
                    w = [(ws, wvv)] if ic == 0 else []
                    if ic == 0 and oevh:
                        w.append(("opV", oevh[-1][qg]))
                    p.op("tensor", "matmul", bank,
                         wqd[par][:, 128 * ic:128 * (ic + 1)],
                         o2_ch(ic)[:, 512 * qg:512 * (qg + 1)],
                         start=(ic == 0), stop=(ic == NC - 1), waits=w,
                         inc=("opT", 1) if ic == NC - 1 else None)
            evs = []
            for qg in range(4):
                xl = xc(oc)[:, 512 * qg:512 * (qg + 1)]
                p.op("vector", "scalar_tensor_tensor", xl, psbank(qg),
                     bcol["o"][:, oc:oc + 1], xl, op0=ALU.add, op1=ALU.add,
                     waits=[("opT", p.val("opT"))] if qg == 0 else [],
                     inc=("opV", 1))
                evs.append(p.val("opV"))
            oevh.append(evs)
        p.barrier()
        dump_chunks("x_l1", xc)

        for tt in range(NTT):
            buf = tt % 2
            dsem = "dO" + str(buf)
            pst = psA if buf == 0 else psB
            w = [("foV", p.val("foV") - 1)] if tt >= 2 else []
            for fc in range(NC):
                p.op("tensor", "transpose",
                     pst[:, 1024 + 128 * fc:1024 + 128 * fc + 128],
                     xc(fc)[:, 128 * tt:128 * tt + 128], identf[:, :],
                     waits=w if fc == 0 else [],
                     inc=("foT", 1) if fc == NC - 1 else None)
            ev = [("foT", p.val("foT"))]
            if tt >= 2:
                ev.append((dsem, 16 * (tt // 2)))
            p.op("vector", "tensor_copy", A2[:, 1024 * buf:1024 * buf + 1024],
                 pst[:, 1024:2048], waits=ev, inc=("foV", 1))
            p.op("sync", "dma_start", out=out_t[128 * tt:128 * (tt + 1), :],
                 in_=A2[:, 1024 * buf:1024 * buf + 1024],
                 waits=[("foV", p.val("foV"))], inc=(dsem, 16))
        p.wait("sync", "dO0", p.val("dO0"))
        p.wait("sync", "dO1", p.val("dO1"))
        p.barrier()
        p.finalize(block)
        return nc, dbg


def _attn_pipeline(p, nslots, kcs_fn, emit_scores, emit_exp, emit_pv,
                   parts_fn, emit_mask=None):
    gesv = []
    prev = None
    for s in range(nslots):
        parts = prev
        if parts is not None:
            parts[0]()
        kcs = kcs_fn(s)
        n = len(kcs)
        pvv = []
        called = set()
        for idx, kc in enumerate(kcs):
            ew = gesv[-2] if len(gesv) >= 2 else None
            emit_scores(s, idx, kc, ew)
            emit_exp(s, idx, kc)
            gesv.append(p.val("aS"))
            if emit_mask is not None:
                pvv.append(emit_mask(s, idx, kc))
            else:
                pvv.append(("aS", gesv[-1]))
            if parts is not None:
                if idx == 2 and 1 not in called:
                    parts[1]()
                    called.add(1)
                if idx == 3 and 2 not in called:
                    parts[2]()
                    called.add(2)
                if idx == 6 and 3 not in called:
                    parts[3]()
                    called.add(3)
            if idx >= 2:
                emit_pv(s, idx - 2, kcs[idx - 2], idx - 2 == 0, False,
                        pvv[idx - 2])
        if parts is not None:
            for i in (1, 2, 3):
                if i not in called:
                    parts[i]()
        emit_pv(s, n - 2, kcs[n - 2], n - 2 == 0, False, pvv[n - 2])
        emit_pv(s, n - 1, kcs[n - 1], False, True, pvv[n - 1])
        if parts is not None:
            parts[4]()
        prev = parts_fn(s)
    for f in prev:
        f()


def _kslice(A1b, pr, kc, row0):
    apx = A1b[:, 8192 + pr * 2048 + 128 * kc: 8192 + pr * 2048 + 128 * kc + 128]
    return apx[row0:row0 + 64, :]


def _qslice(A1b, pr, qg, row0):
    apx = A1b[:, pr * 2048 + 512 * qg: pr * 2048 + 512 * qg + 512]
    return apx[row0:row0 + 64, :]


def _onorm_qd_parts(p, ch, qg, pi, psB, A2, A2f8, identf, identb, scl,
                    odq_ch, o_u, o_q8, dqt_scr, ap3):
    ALUx = ALU
    base = 1024 * pi
    bb = 2048 * pi
    stg8 = A2[:, 14400:15424]
    psBb = psB[:].bitcast(BF16)

    def staging():
        for hh in range(2):
            p.op("vector", "tensor_copy", stg8[0:65, 512 * hh:512 * hh + 512],
                 psB[0:65, base + 512 * hh:base + 512 * hh + 512],
                 waits=[("aT2", p.val("aT2")), ("oT", p.val("oT"))]
                 if hh == 0 else [],
                 inc=("oV", 1) if hh == 1 else None)

    def fwdT():
        for t in range(4):
            for hh in range(2):
                i = 2 * t + hh
                off = base + 65 * i if i < 4 else base + 512 + 65 * (i - 4)
                p.op("tensor", "transpose", psB[:, off:off + 65],
                     stg8[0:65, 512 * hh + 128 * t:512 * hh + 128 * t + 128],
                     identf[0:65, 0:65],
                     waits=[("oV", p.val("oV"))] if i == 0 else [],
                     inc=("oT", 1) if i == 7 else None)

    def chain1():
        p.wait("vector", "oT", p.val("oT"))
        p.op("vector", "reciprocal", scl[:, 16:20],
             ap3(psB, base + 64, [[2048, 128], [65, 4], [1, 1]]), drain=True)
        p.op("vector", "reciprocal", scl[:, 20:24],
             ap3(psB, base + 512 + 64, [[2048, 128], [65, 4], [1, 1]]))
        rzc0 = ap3(scl, 16, [[64, 128], [1, 4], [0, 64]])
        rzc1 = ap3(scl, 20, [[64, 128], [1, 4], [0, 64]])
        p.op("vector", "tensor_tensor",
             ap3(A2, 15424, [[16384, 128], [64, 4], [1, 64]]),
             ap3(psB, base, [[2048, 128], [65, 4], [1, 64]]), rzc0, ALUx.mult,
             drain=True)
        p.op("vector", "tensor_tensor",
             ap3(A2, 15424 + 256, [[16384, 128], [64, 4], [1, 64]]),
             ap3(psB, base + 512, [[2048, 128], [65, 4], [1, 64]]), rzc1,
             ALUx.mult)
        p.op("vector", "reduce_max", scl[:, 24:32], uv_sb(A2, ap3), axis=AX.X,
             apply_absolute_value=True, drain=True)
        p.op("vector", "reduce_max", scl[:, 16:20],
             ap3(scl, 24, [[64, 128], [2, 4], [1, 2]]), axis=AX.X, drain=True)
        p.op("vector", "tensor_single_scalar", scl[:, 16:20], scl[:, 16:20],
             448e-12, ALUx.max, drain=True)
        p.op("vector", "reciprocal", scl[:, 20:24], scl[:, 16:20], drain=True)
        p.op("vector", "tensor_single_scalar", scl[:, 20:24], scl[:, 20:24],
             R224, ALUx.mult, drain=True)
        p.op("vector", "tensor_single_scalar", scl[:, 24:28], scl[:, 16:20],
             INV224, ALUx.mult)
        rt = ap3(scl, 20, [[64, 128], [1, 4], [0, 128]])
        u4 = ap3(A2, 15424, [[16384, 128], [128, 4], [1, 128]])
        p.op("vector", "tensor_tensor", u4, u4, rt, ALUx.mult, drain=True,
             inc=("oV2", 1))

    def chain2():
        st_ = ap3(scl, 24, [[64, 128], [1, 4], [0, 128]])
        p.op("scalar", "activation", o_q8[:, 0:512], o_u[:, 0:512], AF.Copy,
             waits=[("oV2", p.val("oV2"))], drain=True, inc=("oS", 1))
        p.op("vector", "tensor_tensor",
             dqt_scr.rearrange("p (t c) -> p t c", t=4),
             o_q8[:, 0:512].rearrange("p (t c) -> p t c", t=4), st_,
             ALUx.mult,
             waits=[("oS", p.val("oS")), ("oT2", p.val("oT2"))],
             inc=("oV3", 1))

    def backT_evac():
        for t in range(4):
            off = bb + 128 * t if t < 2 else bb + 1024 + 128 * (t - 2)
            p.op("tensor", "transpose", psBb[:, off:off + 128],
                 dqt_scr[:, 128 * t:128 * t + 128], identb[:, :],
                 waits=[("oV3", p.val("oV3"))] if t == 0 else [],
                 inc=("oT2", 1) if t == 3 else None)
        otv = p.val("oT2")
        for hh in range(2):
            p.op("vector", "tensor_copy",
                 odq_ch(ch)[:, 512 * qg + 256 * hh:512 * qg + 256 * hh + 256],
                 psBb[:, bb + 1024 * hh:bb + 1024 * hh + 256],
                 waits=[("oT2", otv)] if hh == 0 else [],
                 inc=("oV4", 1) if hh == 1 else None)

    return [staging, fwdT, chain1, chain2, backT_evac]


def _onorm_plain_parts(p, ch, qg, pi, psB, A2, A2b, identf, identb, scl,
                       dst_ch, ap3):
    ALUx = ALU
    base = 1024 * pi
    bb = 2048 * pi
    stg8 = A2[:, 14400:15424]
    dqt = A2b[:, 30848:31360]
    psBb = psB[:].bitcast(BF16)

    def staging():
        for hh in range(2):
            p.op("vector", "tensor_copy", stg8[0:65, 512 * hh:512 * hh + 512],
                 psB[0:65, base + 512 * hh:base + 512 * hh + 512],
                 waits=[("aT2", p.val("aT2")), ("oT", p.val("oT"))]
                 if hh == 0 else [],
                 inc=("oV", 1) if hh == 1 else None)

    def fwdT():
        for t in range(4):
            for hh in range(2):
                i = 2 * t + hh
                off = base + 65 * i if i < 4 else base + 512 + 65 * (i - 4)
                p.op("tensor", "transpose", psB[:, off:off + 65],
                     stg8[0:65, 512 * hh + 128 * t:512 * hh + 128 * t + 128],
                     identf[0:65, 0:65],
                     waits=[("oV", p.val("oV"))] if i == 0 else [],
                     inc=("oT", 1) if i == 7 else None)

    def chain1():
        p.wait("vector", "oT", p.val("oT"))
        p.op("vector", "reciprocal", scl[:, 16:20],
             ap3(psB, base + 64, [[2048, 128], [65, 4], [1, 1]]), drain=True)
        p.op("vector", "reciprocal", scl[:, 20:24],
             ap3(psB, base + 512 + 64, [[2048, 128], [65, 4], [1, 1]]))
        rzc0 = ap3(scl, 16, [[64, 128], [1, 4], [0, 64]])
        rzc1 = ap3(scl, 20, [[64, 128], [1, 4], [0, 64]])
        p.op("vector", "tensor_tensor",
             dqt.rearrange("p (b c) -> p b c", b=8)[:, 0:4, :],
             ap3(psB, base, [[2048, 128], [65, 4], [1, 64]]), rzc0, ALUx.mult,
             drain=True, waits=[("oT2", p.val("oT2"))])
        p.op("vector", "tensor_tensor",
             dqt.rearrange("p (b c) -> p b c", b=8)[:, 4:8, :],
             ap3(psB, base + 512, [[2048, 128], [65, 4], [1, 64]]), rzc1,
             ALUx.mult, inc=("oV3", 1))

    def chain2():
        pass

    def backT_evac():
        for t in range(4):
            off = bb + 128 * t if t < 2 else bb + 1024 + 128 * (t - 2)
            p.op("tensor", "transpose", psBb[:, off:off + 128],
                 dqt[:, 128 * t:128 * t + 128], identb[:, :],
                 waits=[("oV3", p.val("oV3"))] if t == 0 else [],
                 inc=("oT2", 1) if t == 3 else None)
        otv = p.val("oT2")
        for hh in range(2):
            p.op("vector", "tensor_copy",
                 dst_ch(ch)[:, 512 * qg + 256 * hh:512 * qg + 256 * hh + 256],
                 psBb[:, bb + 1024 * hh:bb + 1024 * hh + 256],
                 waits=[("oT2", otv)] if hh == 0 else [],
                 inc=("oV4", 1) if hh == 1 else None)

    return [staging, fwdT, chain1, chain2, backT_evac]


def uv_sb(A2, ap3):
    return ap3(A2, 15424, [[16384, 128], [64, 8], [1, 64]])


def _emit_out_zero(p, out_t, A2):
    p.barrier()
    p.op("vector", "memset", A2[:, 0:1024], 0.0, drain=True, inc=("zz", 1))
    for tt in range(NTT):
        p.op("sync", "dma_start", out=out_t[128 * tt:128 * (tt + 1), :],
             in_=A2[:, 0:1024], waits=[("zz", p.val("zz"))], inc=("dO", 16))
    p.wait("sync", "dO", p.val("dO"))


_CACHE = {}


def _prep_inputs(inputs):
    c = make_consts()
    w = {}
    w["wqkv_r"] = colchunks(np.asarray(inputs["L0_attn_W_qkv"], np.float32), 24)
    w["bqkv_cols"] = np.ascontiguousarray(
        np.asarray(inputs["L0_attn_b_qkv"], np.float32).reshape(24, 128).T)
    w["wout_r"] = colchunks(np.asarray(inputs["L0_attn_W_out"], np.float32), 8)
    w["bout_cols"] = np.ascontiguousarray(
        np.asarray(inputs["L0_attn_b_out"], np.float32).reshape(8, 128).T)
    wffn = np.asarray(inputs["L0_ffn_weight"], np.float32)
    w["w_ffn"] = wffn
    w["w_ffnT"] = np.ascontiguousarray(wffn.T)
    for k, nm in (("wq", "L1_attn_wq"), ("wk", "L1_attn_wk"),
                  ("wv", "L1_attn_wv"), ("wo", "L1_attn_wo")):
        w[f"l1{k}_r"] = colchunks(np.asarray(inputs[nm], np.float32), 8)
    for k, nm in (("q", "L1_attn_bq"), ("k", "L1_attn_bk"),
                  ("v", "L1_attn_bv"), ("o", "L1_attn_bo")):
        w[f"b{k}_cols"] = np.ascontiguousarray(
            np.asarray(inputs[nm], np.float32).reshape(8, 128).T)
    w["wg_cols"] = np.ascontiguousarray(
        np.asarray(inputs["L1_ffn_Wg"], np.float32).reshape(8, 128).T)
    for k, nm in (("Wq", "L1_ffn_Wq"), ("Wk", "L1_ffn_Wk"),
                  ("Wv", "L1_ffn_Wv")):
        w[f"f{k}_r"] = colchunks(np.asarray(inputs[nm], np.float32), 8)
    w["fWo_nat"] = np.ascontiguousarray(
        np.asarray(inputs["L1_ffn_Wo"], np.float32))
    w.update(c)
    return w


def kernel(**inputs):
    key = "final"
    if key not in _CACHE:
        _CACHE[key] = build_program(stage=STAGE_FINAL)
    nc, dbg = _CACHE[key]
    shared = _prep_inputs(inputs)
    x = np.asarray(inputs["L0_attn_x"], np.float32)
    in_maps = []
    for b in range(8):
        m = dict(shared)
        m["x_in"] = np.ascontiguousarray(x[b])
        in_maps.append(m)
    res = run_bass_kernel_spmd(nc, in_maps, core_ids=list(range(8)))
    out = np.stack([res.results[b]["out"] for b in range(8)], axis=0)
    return out


# version 16 (final)
# speedup vs baseline: 1.0989x; 1.0326x over previous
import numpy as np
import concourse.bass as bass
import concourse.mybir as mybir
from concourse.bass_utils import run_bass_kernel_spmd
from contextlib import ExitStack

F32 = mybir.dt.float32
BF16 = mybir.dt.bfloat16
F8 = mybir.dt.float8e4
I32 = mybir.dt.int32
AF = mybir.ActivationFunctionType
ALU = mybir.AluOpType
AX = mybir.AxisListType

S = 2048
D = 1024
NH = 16
DH = 64
NC = 8
NTT = 16
EPS = 1e-5
ENGINES = ("tensor", "vector", "scalar", "gpsimd", "sync")
R224 = 224.0
INV224 = float(np.float32(1.0 / 224.0))
INV7 = float(np.float32(1.0 / 7.0))
INVD = float(np.float32(1.0 / 1024.0))
STAGE_FINAL = 99


def _hadamard(n):
    h = np.array([[1.0]], dtype=np.float64)
    while h.shape[0] < n:
        h = np.block([[h, h], [h, -h]])
    return (h / np.sqrt(n)).astype(np.float32)


def make_consts():
    c = {}
    c["c_identf"] = np.eye(128, dtype=np.float32)
    c["c_identb"] = np.eye(128, dtype=np.float32)
    c["c_onesrow"] = np.ones((1, 128), dtype=np.float32)
    c["c_onescol"] = np.ones((128, 1), dtype=np.float32)
    c["c_ones128"] = np.ones((128, 128), dtype=np.float32)
    c["c_iotac"] = np.arange(128, dtype=np.float32).reshape(128, 1)
    c["c_iotar"] = np.broadcast_to(np.arange(1024, dtype=np.float32),
                                   (128, 1024)).copy()
    c["c_hadam"] = _hadamard(D)
    kr = np.arange(128)[:, None]
    qf = np.arange(512)[None, :]
    masks = [(np.abs(off + kr - qf) <= 256).astype(np.float32)
             for off in range(-256, 768, 128)]
    c["c_lmasks"] = np.concatenate(masks, axis=1)
    c["c_tri"] = (np.arange(128)[:, None] < np.arange(128)[None, :]).astype(
        np.float32)
    return c


def colchunks(w, nblocks):
    return np.ascontiguousarray(
        w.reshape(8, 128, nblocks, 128).transpose(2, 1, 0, 3)
        .reshape(nblocks, 128, 1024))


class PB:
    def __init__(self, nc, ctx):
        self.nc = nc
        self.ctx = ctx
        self.q = {e: [] for e in ENGINES}
        self.sems = {}
        self.cnt = {}

    def sem(self, name):
        if name not in self.sems:
            self.sems[name] = self.ctx.enter_context(self.nc.semaphore(name))
            self.cnt[name] = 0
        return self.sems[name]

    def val(self, name):
        self.sem(name)
        return self.cnt[name]

    def bump(self, name, by=1):
        self.sem(name)
        self.cnt[name] += by
        return self.cnt[name]

    def op(self, eng, meth, *args, waits=(), inc=None, drain=False, **kw):
        import sys as _sys
        fr = _sys._getframe(1)
        tag = f"{fr.f_code.co_name}:{fr.f_lineno}"
        fr2 = fr.f_back
        if fr2 is not None:
            tag += f"<{fr2.f_code.co_name}:{fr2.f_lineno}"
        wl = [(self.sem(s), v) for (s, v) in waits]
        ip = None
        if inc is not None:
            s, by = inc
            self.bump(s, by)
            ip = (self.sems[s], by)

        def fn(e, meth=meth, args=args, kw=kw, wl=wl, ip=ip, drain=drain,
               tag=tag):
            for s, v in wl:
                e.wait_ge(s, v)
            if drain:
                e.drain()
            ins = getattr(e, meth)(*args, **kw)
            ins.annotate(tag)
            if ip is not None:
                ins.then_inc(*ip)
        self.q[eng].append(fn)

    def wait(self, eng, name, v):
        s = self.sem(name)
        self.q[eng].append(lambda e, s=s, v=v: e.wait_ge(s, v))

    def barrier(self, engines=ENGINES):
        targets = {}
        for e in engines:
            targets[e] = self.bump("bar_" + e)
        for e in engines:
            s = self.sems["bar_" + e]

            def fin(eng, s=s):
                eng.drain().then_inc(s, 1)
            self.q[e].append(fin)
        for e in engines:
            for o in engines:
                if o != e:
                    self.wait(e, "bar_" + o, targets[o])

    def finalize(self, block):
        for e in ENGINES:
            fns = self.q[e]

            def body(eng, fns=fns):
                for f in fns:
                    f(eng)
            getattr(block, e)(body)


def ap3(t, off, dims):
    return bass.AP(t, off, dims)


def build_program(stage=STAGE_FINAL, debug_taps=()):
    nc = bass.Bass("TRN2", target_bir_lowering=False)
    taps = set(debug_taps)

    def din(name, shape, dt=F32):
        return nc.dram_tensor(name, list(shape), dt, kind="ExternalInput").ap()

    x_in = din("x_in", (S, D))
    wqkv_r = din("wqkv_r", (24, 128, 1024))
    bqkv_cols = din("bqkv_cols", (128, 24))
    wout_r = din("wout_r", (8, 128, 1024))
    bout_cols = din("bout_cols", (128, 8))
    w_ffnT = din("w_ffnT", (D, D))
    w_ffn = din("w_ffn", (D, D))
    l1w_r = {k: din(f"l1{k}_r", (8, 128, 1024)) for k in ("wq", "wk", "wv", "wo")}
    bcols_in = {k: din(f"b{k}_cols", (128, 8)) for k in ("q", "k", "v", "o")}
    wg_cols = din("wg_cols", (128, 8))
    fw_r = {k: din(f"f{k}_r", (8, 128, 1024)) for k in ("Wq", "Wk", "Wv")}
    fWo_nat = din("fWo_nat", (D, D))
    c_identf = din("c_identf", (128, 128))
    c_identb = din("c_identb", (128, 128))
    c_onesrow = din("c_onesrow", (1, 128))
    c_onescol = din("c_onescol", (128, 1))
    c_ones128 = din("c_ones128", (128, 128))
    c_tri = din("c_tri", (128, 128))
    c_iotac = din("c_iotac", (128, 1))
    c_iotar = din("c_iotar", (128, 1024))
    c_hadam = din("c_hadam", (D, D))
    c_lmasks = din("c_lmasks", (128, 8 * 512))

    out_t = nc.dram_tensor("out", [S, D], F32, kind="ExternalOutput").ap()

    dbg = {}

    def mkdbg(name, shape):
        if name in taps:
            dbg[name] = nc.dram_tensor("dbg_" + name, list(shape), F32,
                                       kind="ExternalOutput").ap()
    for nm, shp in [("xT", (128, NC * S)), ("xn", (128, NC * S)),
                    ("khat", (128, NC * S)), ("srow", (1, S)),
                    ("x_ffn", (128, NC * S)), ("xdq", (128, NC * S)),
                    ("qT", (128, 4 * S)), ("kT", (128, 4 * S)),
                    ("vaug", (128, NTT * 8 * 65)), ("e0", (128, 1024)),
                    ("odq", (128, NC * S)), ("x_l0", (128, NC * S)),
                    ("xn2", (128, NC * S)), ("lrow", (1, S)),
                    ("locol", (128, 1)), ("rankp", (128, 16)),
                    ("attn_in", (128, 8 * 1024)), ("lsel", (1, 1024)),
                    ("y2g", (128, 8 * 1024)), ("x_ffn2", (128, NC * S)),
                    ("o2", (128, NC * S)), ("x_l1", (128, NC * S)),
                    ("tern", (128, NC * 1024)), ("socol", (128, 8)),
                    ("gcols", (128, 8)), ("selp", (128, 16))]:
        mkdbg(nm, shp)

    with ExitStack() as ctx:
        p = PB(nc, ctx)

        def sbt(name, shape, dt=F32):
            return ctx.enter_context(nc.sbuf_tensor(name, list(shape), dt))

        AXr = sbt("AX", (128, 16384))
        A1 = sbt("A1", (128, 16384))
        A2 = sbt("A2", (128, 16384))
        identf = sbt("identf", (128, 128))
        identb = sbt("identb", (128, 128), BF16)
        onesrow = sbt("onesrow", (1, 128))
        onescol = sbt("onescol", (128, 1))
        tri_b = sbt("tri_b", (128, 128), BF16)
        tri_f = sbt("tri_f", (128, 128))
        iotac = sbt("iotac", (128, 1))
        iotar = sbt("iotar", (128, 1024))
        bqkv = sbt("bqkv", (128, 24))
        bout = sbt("bout", (128, 8))
        bcol = {k: sbt("b_" + k, (128, 8)) for k in ("q", "k", "v", "o")}
        wg = sbt("wg", (128, 8))
        wgh = sbt("wgh", (128, 8), BF16)
        wgl = sbt("wgl", (128, 8), BF16)
        scl = sbt("scl", (128, 64))
        sclb = scl[:].bitcast(BF16)
        ebuf = sbt("ebuf", (128, 2048), BF16)
        wqd = [sbt("wqd0", (128, 1024), BF16), sbt("wqd1", (128, 1024), BF16)]
        w8 = sbt("w8", (128, 1024), F8)

        psA = ctx.enter_context(nc.psum_tensor("psA", [128, 2048], F32))
        psB = ctx.enter_context(nc.psum_tensor("psB", [128, 2048], F32))

        block = ctx.enter_context(nc.Block())

        A1b = A1[:].bitcast(BF16)
        A2b = A2[:].bitcast(BF16)
        A2f8 = A2[:].bitcast(F8)
        wldf = [A2[:, 12352:13376], A2[:, 13376:14400]]
        hbuf = wqd

        def xc(c):
            return AXr[:, 2048 * c:2048 * (c + 1)]

        def a1f(c):
            return A1[:, 2048 * c:2048 * (c + 1)]

        def bc_row(dst_ps, row_ap, waits=(), inc=None):
            n = row_ap.shape[-1]
            nseg = (n + 511) // 512
            for s in range(nseg):
                a, b = 512 * s, min(512 * (s + 1), n)
                p.op("tensor", "matmul", dst_ps[:, a:b], onesrow[0:1, :],
                     row_ap[0:1, a:b], start=True, stop=True,
                     waits=waits if s == 0 else (),
                     inc=inc if s == nseg - 1 else None)

        def flat_dma(dst_off, src_ap, ncols, waits=()):
            for c in range(ncols):
                p.op("sync", "dma_start",
                     out=ap3(A2, dst_off + 128 * c, [[16384, 1], [1, 128]]),
                     in_=src_ap[:, c:c + 1], waits=waits if c == 0 else (),
                     inc=("dF", 16))

        def transpose_tiles(pairs, ident, tsem, vsem, first_waits=()):
            bf = pairs[0][0].dtype == BF16
            nw = (len(pairs) + 7) // 8
            for wv in range(nw):
                chunk = pairs[8 * wv:8 * wv + 8]
                pst = psA if wv % 2 == 0 else psB
                pstv = pst[:].bitcast(BF16) if bf else pst
                base = 2048 if bf else 1024
                w = [(vsem, p.val(vsem) - 1)] if wv >= 2 else []
                if wv == 0:
                    w = w + list(first_waits)
                for j, (sap, dap) in enumerate(chunk):
                    pt = pstv[:, base + 128 * j:base + 128 * j + 128]
                    p.op("tensor", "transpose", pt, sap, ident,
                         waits=w if j == 0 else [],
                         inc=(tsem, 1) if j == len(chunk) - 1 else None)
                tv = p.val(tsem)
                j0 = 0
                while j0 < len(chunk):
                    j1 = j0
                    while (j1 + 1 < len(chunk)
                           and chunk[j0][1].ndim == 2
                           and chunk[j1 + 1][1].ndim == 2
                           and chunk[j1 + 1][1].tensor is chunk[j0][1].tensor
                           and chunk[j1 + 1][1].offset
                           == chunk[j0][1].offset + 128 * (j1 + 1 - j0)):
                        j1 += 1
                    n = j1 - j0 + 1
                    dst = chunk[j0][1]
                    if n == 1:
                        dsta = dst
                    else:
                        dsta = bass.AP(dst.tensor, dst.offset,
                                       [list(dst.ap[0]), [1, 128 * n]])
                    p.op("vector", "tensor_copy", dsta,
                         pstv[:, base + 128 * j0:base + 128 * j0 + 128 * n],
                         waits=[(tsem, tv)] if j0 == 0 else [],
                         inc=(vsem, 1) if j1 == len(chunk) - 1 else None)
                    j0 = j1 + 1

        def transpose_reduce(srcs, dst_cols_fn, tsem, vsem, first_waits=()):
            nw = (len(srcs) + 7) // 8
            for wv in range(nw):
                chunk = srcs[8 * wv:8 * wv + 8]
                pst = psA if wv % 2 == 0 else psB
                w = [(vsem, p.val(vsem) - 1)] if wv >= 2 else []
                if wv == 0:
                    w = w + list(first_waits)
                for j, sap in enumerate(chunk):
                    pt = pst[:, 1024 + 128 * j:1024 + 128 * j + 128]
                    p.op("tensor", "transpose", pt, sap, identf[:, :],
                         waits=w if j == 0 else [],
                         inc=(tsem, 1) if j == len(chunk) - 1 else None)
                nj = len(chunk)
                p.op("vector", "reduce_max", dst_cols_fn(wv),
                     pst[:, 1024:1024 + 128 * nj].rearrange(
                         "p (j c) -> p j c", j=nj),
                     axis=AX.X, apply_absolute_value=True,
                     waits=[(tsem, p.val(tsem))],
                     inc=(vsem, 1))

        def dump_chunks(name, chfn, n=NC, width=2048):
            if name in taps:
                p.barrier()
                for c2 in range(n):
                    p.op("gpsimd", "dma_start",
                         out=dbg[name][:, width * c2:width * (c2 + 1)],
                         in_=chfn(c2), waits=[("dDbg", p.val("dDbg"))],
                         inc=("dDbg", 16))
                p.wait("gpsimd", "dDbg", p.val("dDbg"))
                p.barrier()

        for dst, src in [(identf, c_identf), (onesrow, c_onesrow),
                         (onescol, c_onescol), (iotac, c_iotac),
                         (iotar, c_iotar), (bqkv, bqkv_cols), (bout, bout_cols),
                         (bcol["q"], bcols_in["q"]), (bcol["k"], bcols_in["k"]),
                         (bcol["v"], bcols_in["v"]), (bcol["o"], bcols_in["o"]),
                         (wg, wg_cols)]:
            p.op("sync", "dma_start", out=dst[:], in_=src[:, :], inc=("dIn", 16))
        p.op("gpsimd", "dma_start", out=identb[:], in_=c_identb[:, :],
             inc=("dInG", 16))
        p.op("gpsimd", "dma_start", out=tri_b[:], in_=c_tri[:, :],
             inc=("dInG", 16))
        p.op("sync", "dma_start", out=tri_f[:], in_=c_tri[:, :],
             inc=("dIn", 16))
        for tt in range(NTT):
            p.op("sync", "dma_start", out=A1[:, 1024 * tt:1024 * (tt + 1)],
                 in_=x_in[128 * tt:128 * (tt + 1), :], inc=("dIn", 16))
        p.wait("tensor", "dIn", p.val("dIn"))
        p.wait("vector", "dIn", p.val("dIn"))
        p.wait("vector", "dInG", p.val("dInG"))
        p.wait("scalar", "dIn", p.val("dIn"))

        for ic in range(NC):
            if ic < 4:
                hdst = A2b[:, 20480 + 1024 * ic:20480 + 1024 * (ic + 1)]
            elif ic < 6:
                hdst = ebuf[:, 1024 * (ic - 4):1024 * (ic - 3)]
            else:
                hdst = wqd[ic - 6][:]
            p.op("gpsimd", "dma_start", out=hdst,
                 in_=c_hadam[128 * ic:128 * (ic + 1), :], inc=("dH0", 16))
        p.op("vector", "tensor_copy", wgh[:], wg[:])
        p.op("vector", "tensor_tensor", wgl[:], wg[:], wgh[:], ALU.subtract,
             drain=True)

        pairs = []
        for fc in range(NC):
            for tt in range(NTT):
                src = A1[:, 1024 * tt + 128 * fc:1024 * tt + 128 * fc + 128]
                dst = xc(fc)[:, 128 * tt:128 * tt + 128]
                pairs.append((src, dst))
        transpose_tiles(pairs, identf[:, :], "tT", "tV")
        p.barrier()
        dump_chunks("xT", xc)

        def emit_layernorm(dst_fn, rows_off, sq_off):
            r_row = A2[0:1, rows_off:rows_off + 2048]
            a_row = A2[0:1, rows_off + 2048:rows_off + 4096]
            m_row = A2[0:1, rows_off + 4096:rows_off + 6144]
            t_row = A2[0:1, rows_off + 6144:rows_off + 8192]
            for c in range(NC):
                for g in range(4):
                    p.op("tensor", "matmul", psA[0:1, 512 * g:512 * (g + 1)],
                         onescol[:, 0:1], xc(c)[:, 512 * g:512 * (g + 1)],
                         start=(c == 0), stop=(c == NC - 1),
                         inc=("lnt", 1) if (c == NC - 1 and g == 3) else None)
            for c in range(NC):
                slot = c % 2
                sq = A2[:, sq_off + 2048 * slot: sq_off + 2048 * (slot + 1)]
                w = [("lnt2", p.val("lnt2") - 1)] if c >= 2 else []
                p.op("scalar", "activation", sq, xc(c), AF.Square, drain=True,
                     waits=w, inc=("lns", 1))
                for g in range(4):
                    p.op("tensor", "matmul", psB[0:1, 512 * g:512 * (g + 1)],
                         onescol[:, 0:1], sq[:, 512 * g:512 * (g + 1)],
                         start=(c == 0), stop=(c == NC - 1),
                         waits=[("lns", p.val("lns"))] if g == 0 else [],
                         inc=("lnt2", 1) if g == 3 else None)
            p.op("vector", "tensor_single_scalar", m_row, psA[0:1, :], INVD,
                 ALU.mult, waits=[("lnt", p.val("lnt")),
                                  ("lnt2", p.val("lnt2"))], drain=True)
            p.op("vector", "tensor_single_scalar", r_row, psB[0:1, :], INVD,
                 ALU.mult, drain=True)
            p.op("vector", "tensor_mul", a_row, m_row, m_row, drain=True)
            p.op("vector", "tensor_tensor", r_row, r_row, a_row, ALU.subtract,
                 drain=True)
            p.op("vector", "tensor_single_scalar", r_row, r_row, EPS, ALU.add,
                 drain=True, inc=("lnv", 1))
            p.op("scalar", "activation", a_row, r_row, AF.Ln,
                 waits=[("lnv", p.val("lnv"))], drain=True)
            p.op("scalar", "activation", a_row, a_row, AF.Exp, scale=-0.5,
                 drain=True, inc=("lns", 1))
            p.op("vector", "tensor_mul", t_row, a_row, a_row,
                 waits=[("lns", p.val("lns"))], drain=True)
            p.op("vector", "tensor_mul", t_row, t_row, r_row, drain=True)
            p.op("vector", "tensor_scalar", t_row, t_row, -0.5, 1.5,
                 op0=ALU.mult, op1=ALU.add, drain=True)
            p.op("vector", "tensor_mul", r_row, a_row, t_row, drain=True)
            p.op("vector", "tensor_mul", a_row, m_row, r_row, drain=True)
            p.op("vector", "tensor_single_scalar", a_row, a_row, -1.0, ALU.mult,
                 drain=True, inc=("lnv", 1))
            p.barrier(("tensor", "vector"))
            for half in range(2):
                cols = slice(1024 * half, 1024 * (half + 1))
                bc_row(psA[:, 0:1024], r_row[0:1, cols])
                bc_row(psA[:, 1024:2048], a_row[0:1, cols], inc=("lnt", 1))
                for c in range(NC):
                    dst = dst_fn(c)[:, cols]
                    p.op("vector", "tensor_tensor", dst, xc(c)[:, cols],
                         psA[:, 0:1024], ALU.mult,
                         waits=[("lnt", p.val("lnt"))] if c == 0 else [],
                         drain=(c == 0))
                    p.op("vector", "tensor_add", dst, dst, psA[:, 1024:2048],
                         drain=True)
                p.barrier(("tensor", "vector"))

        emit_layernorm(a1f, rows_off=0, sq_off=12288)
        p.barrier()
        dump_chunks("xn", a1f)
        if stage <= 1:
            _emit_out_zero(p, out_t, A2)
            p.finalize(block)
            return nc, dbg

        xh = A2[:, 8192:10240]
        srow = A2[0:1, 10240:12288]
        rrow = A2[0:1, 12544:12800]
        hi_g = A2b[:, 2 * 12800:2 * 12800 + 2048]
        lo_g = A2b[:, 2 * 13824:2 * 13824 + 2048]
        sgn = A2b[:, 29696:31744]

        def khat_ch(c):
            return A2b[:, 2048 * c:2048 * (c + 1)]

        def psbank(i, n=512):
            return (psA if i < 4 else psB)[:, 512 * (i % 4):512 * (i % 4) + n]

        a1_3 = A1[:, 0:16384].rearrange("p (c t) -> p c t", c=8)
        hi3 = hi_g[:, 0:2048].rearrange("p (c t) -> p c t", c=8)
        lo3 = lo_g[:, 0:2048].rearrange("p (c t) -> p c t", c=8)
        kh3 = A2b[:, 0:16384].rearrange("p (c t) -> p c t", c=8)
        usl3 = xh[:, 0:2048].rearrange("p (o t) -> p o t", o=8)
        sgn3 = sgn[:, 0:2048].rearrange("p (o t) -> p o t", o=8)
        xh3 = ap3(A2, 8192, [[16384, 128], [256, 8], [1, 256]])
        rbc = ap3(psB, 256, [[2048, 128], [0, 8], [1, 256]])

        def hslab(ic):
            if ic < 4:
                return A2b[:, 20480 + 1024 * ic:20480 + 1024 * (ic + 1)]
            if ic < 6:
                return ebuf[:, 1024 * (ic - 4):1024 * (ic - 3)]
            return wqd[ic - 6][:]

        ffv_hl = {}
        ffv_x = {}
        for g in range(9):
            if g >= 1:
                gp = g - 1
                ew = [("ffLd", 8 * g)]
                if gp >= 1:
                    ew.append(("ffU", 2 * gp))
                    ew.append(("ffT2", p.val("ffT2")))
                p.op("scalar", "activation", xh, psA[:, 0:2048], AF.Copy,
                     waits=ew, drain=True, inc=("ffE2", 1))
                for wv in range(2):
                    for j in range(NC):
                        w = []
                        if wv == 0 and j == 0:
                            w.append(("ffE2", g))
                            if gp >= 1:
                                w.append(("ffG", gp))
                                w.append(("ffU", 2 * gp))
                        p.op("tensor", "transpose",
                             psB[:, 1024 * wv + 128 * j:
                                 1024 * wv + 128 * j + 128],
                             xh[:, 256 * j + 128 * wv:256 * j + 128 * wv + 128],
                             identf[:, :], waits=w,
                             inc=("ffT2", 1) if j == NC - 1 else None)
                for wv in range(2):
                    p.op("vector", "reduce_max",
                         scl[:, 16 + 8 * wv:24 + 8 * wv],
                         psB[:, 1024 * wv:1024 * wv + 1024].rearrange(
                             "p (j c) -> p j c", j=8),
                         axis=AX.X, apply_absolute_value=True,
                         waits=[("ffT2", p.val("ffT2"))] if wv == 0 else [],
                         drain=(wv == 0))
                for sub in range(2):
                    p.op("vector", "reduce_max", scl[:, sub:sub + 1],
                         scl[:, 16 + 8 * sub:24 + 8 * sub], axis=AX.X,
                         drain=(sub == 0),
                         inc=("ffG", 1) if sub == 1 else None)
                p.op("vector", "tensor_single_scalar", scl[:, 0:2],
                     scl[:, 0:2], 1e-5, ALU.max, drain=True)
                p.op("vector", "reciprocal", scl[:, 2:4], scl[:, 0:2],
                     drain=True)
                p.op("vector", "tensor_single_scalar", scl[:, 2:4],
                     scl[:, 2:4], 7.0, ALU.mult, drain=True)
                p.op("vector", "tensor_single_scalar",
                     scl[:, 32 + 2 * gp:34 + 2 * gp], scl[:, 0:2], INV7,
                     ALU.mult, inc=("ffV", 1))
                for j in range(2):
                    p.op("tensor", "transpose",
                         psB[0:1, 128 * j:128 * j + 128],
                         scl[:, 2 + j:3 + j], identf[:, :],
                         waits=[("ffV", p.val("ffV"))] if j == 0 else [],
                         inc=("ffT3", 1) if j == 1 else None)
                p.op("vector", "tensor_copy", rrow, psB[0:1, 0:256],
                     waits=[("ffT3", p.val("ffT3"))], drain=True,
                     inc=("ffV", 1))
                bc_row(psB[:, 256:512], rrow[0:1, :],
                       waits=[("ffV", p.val("ffV"))], inc=("ffT3", 1))
            if g < 8:
                tgn = slice(256 * g, 256 * (g + 1))
                w = [("ffLd", 8 * g)] if g > 0 else []
                p.op("vector", "tensor_copy", hi3, a1_3[:, :, tgn],
                     drain=True, waits=w)
                p.op("vector", "tensor_tensor", lo3, a1_3[:, :, tgn], hi3,
                     ALU.subtract, drain=True, inc=("ffV", 1))
                ffv_hl[g] = p.val("ffV")
                for oc in range(NC):
                    dst = psA[:, 256 * oc:256 * (oc + 1)]
                    for ic in range(NC):
                        buf = hslab(ic)
                        w = []
                        if oc == 0 and ic == 0:
                            w.append(("ffV", ffv_hl[g]))
                            if g >= 1:
                                w.append(("ffE2", g))
                            if g == 0:
                                w.append(("dH0", p.val("dH0")))
                        p.op("tensor", "matmul", dst,
                             buf[:, 128 * oc:128 * (oc + 1)],
                             hi_g[:, 256 * ic:256 * (ic + 1)],
                             start=(ic == 0), stop=False, waits=w)
                        p.op("tensor", "matmul", dst,
                             buf[:, 128 * oc:128 * (oc + 1)],
                             lo_g[:, 256 * ic:256 * (ic + 1)],
                             start=False, stop=(ic == NC - 1),
                             inc=("ffLd", 1) if ic == NC - 1 else None)
            if g >= 1:
                gp = g - 1
                tg = slice(256 * gp, 256 * (gp + 1))
                rb4 = ap3(psB, 256, [[2048, 128], [0, 4], [1, 256]])
                u_h = A2[:, 14848:15872]
                sg_h = A2b[:, 31744:32768]
                u3 = u_h.rearrange("p (o t) -> p o t", o=4)
                sg3h = sg_h.rearrange("p (o t) -> p o t", o=4)
                for hh in range(2):
                    eng = "vector"
                    w0 = []
                    if hh == 0:
                        w0.append(("ffT3", p.val("ffT3")))
                        if gp >= 1:
                            w0.append(("ffK", gp))
                    p.op("vector", "tensor_tensor", u3,
                         ap3(A2, 8192 + 1024 * hh,
                             [[16384, 128], [256, 4], [1, 256]]),
                         rb4, ALU.mult,
                         waits=w0, drain=True, inc=("ffU", 1))
                    p.op("scalar", "activation", sg_h, u_h, AF.Sign,
                         waits=[("ffU", p.val("ffU"))], drain=True)
                    p.op("scalar", "activation", u_h, u_h, AF.Abs,
                         drain=True, inc=("ffS", 1))
                    ksl3 = kh3[:, 4 * hh:4 * hh + 4, tg]
                    fw = [("ffS", p.val("ffS"))]
                    p.op(eng, "tensor_single_scalar", ksl3, u3, 0.5,
                         ALU.is_ge, drain=True, waits=fw)
                    for k in range(1, 7):
                        p.op(eng, "scalar_tensor_tensor", ksl3, u3,
                             k + 0.5, ksl3, op0=ALU.is_ge, op1=ALU.add,
                             drain=True)
                    p.op(eng, "tensor_tensor", ksl3, ksl3, sg3h,
                         ALU.mult, drain=True,
                         inc=("ffK", 1) if hh == 1 else None)
        p.barrier()

        dump_chunks("khat", khat_ch)
        if "srow" in taps:
            p.op("gpsimd", "dma_start", out=dbg["srow"][:, :], in_=srow,
                 waits=[("dDbg", p.val("dDbg"))], inc=("dDbg", 16))
            p.wait("gpsimd", "dDbg", p.val("dDbg"))
            p.barrier()

        for j in range(16):
            pst = psA if j < 8 else psB
            p.op("tensor", "transpose",
                 pst[0:1, 1024 + 128 * (j % 8):1152 + 128 * (j % 8)],
                 scl[:, 32 + j:33 + j], identf[:, :],
                 inc=("ffT4", 1) if j == 15 else None)
        p.op("vector", "tensor_copy", srow[0:1, 0:1024], psA[0:1, 1024:2048],
             waits=[("ffT4", p.val("ffT4"))], drain=True)
        p.op("vector", "tensor_copy", srow[0:1, 1024:2048], psB[0:1, 1024:2048],
             inc=("ffV", 1))
        for half in range(2):
            cols = slice(1024 * half, 1024 * (half + 1))
            bc_row(psA[:, 0:1024], srow[0:1, cols],
                   waits=[("ffV", p.val("ffV"))] if half == 0 else [],
                   inc=("ffT4", 1))
            for c in range(NC):
                p.op("vector", "tensor_tensor", khat_ch(c)[:, cols],
                     khat_ch(c)[:, cols], psA[:, 0:1024], ALU.mult,
                     waits=[("ffT4", p.val("ffT4"))] if c == 0 else [],
                     drain=(c == 0))
            p.barrier(("tensor", "vector"))

        so_cols = scl[:, 8:16]
        wldf_t = [A2[:, 14336:15360], A2[:, 15360:16384]]
        for oc in range(NC):
            buf = wldf_t[oc % 2]
            ws = "dW" + str(oc % 2)
            w = [("twV", p.val("twV") - 1)] if oc >= 2 else []
            p.op("sync", "dma_start", out=buf[:],
                 in_=w_ffn[128 * oc:128 * (oc + 1), :], waits=w, inc=(ws, 16))
            p.op("vector", "reduce_sum", so_cols[:, oc:oc + 1], buf[:],
                 axis=AX.X, apply_absolute_value=True,
                 waits=[(ws, p.val(ws))], drain=True)
            p.op("vector", "tensor_single_scalar", so_cols[:, oc:oc + 1],
                 so_cols[:, oc:oc + 1], INVD, ALU.mult, drain=True)
            p.op("vector", "tensor_single_scalar", so_cols[:, oc:oc + 1],
                 so_cols[:, oc:oc + 1], 1e-5, ALU.max, drain=True,
                 inc=("twV", 1))
        if "socol" in taps:
            p.barrier()
            p.op("gpsimd", "dma_start", out=dbg["socol"][:, :], in_=so_cols,
                 waits=[("dDbg", p.val("dDbg"))], inc=("dDbg", 16))
            p.wait("gpsimd", "dDbg", p.val("dDbg"))
        so_row = A2[0:1, 12288:13312]
        for j in range(8):
            p.op("tensor", "transpose", psA[0:1, 1024 + 128 * j:1152 + 128 * j],
                 so_cols[:, j:j + 1], identf[:, :],
                 waits=[("twV", p.val("twV"))] if j == 0 else [],
                 inc=("twT", 1) if j == 7 else None)
        p.op("vector", "tensor_copy", so_row, psA[0:1, 1024:2048],
             waits=[("twT", p.val("twT"))], drain=True, inc=("twV", 1))
        hs_row = A2[0:1, 13312:14336]
        p.op("vector", "tensor_single_scalar", hs_row, so_row, 0.5, ALU.mult,
             drain=True, inc=("twV", 1))
        bc_row(psA[:, 0:1024], hs_row[0:1, :], waits=[("twV", p.val("twV"))],
               inc=("twT", 1))
        p.wait("vector", "twT", p.val("twT"))
        p.op("vector", "tensor_single_scalar", hs_row, hs_row, -1.0, ALU.mult,
             drain=True, inc=("twV", 1))
        bc_row(psA[:, 1024:2048], hs_row[0:1, :], waits=[("twV", p.val("twV"))],
               inc=("twT", 1))
        p.barrier(("tensor", "vector", "sync"))

        def ternT_ch(ic):
            return A2b[:, 16384 + 1024 * ic:16384 + 1024 * (ic + 1)]

        wldf_t = [A2[:, 14336:15360], A2[:, 15360:16384]]
        for ic in range(NC):
            buf = wldf_t[ic % 2]
            ws = "dW" + str(ic % 2)
            w = [("twV", p.val("twV") - 1)] if ic >= 2 else []
            p.op("sync", "dma_start", out=buf[:],
                 in_=w_ffnT[128 * ic:128 * (ic + 1), :], waits=w,
                 inc=(ws, 16))
            p.op("vector", "tensor_tensor", wqd[0][:], buf[:], psA[:, 0:1024],
                 ALU.is_gt, waits=[(ws, p.val(ws))], drain=True)
            p.op("vector", "tensor_tensor", wqd[1][:], buf[:], psA[:, 1024:2048],
                 ALU.is_lt, drain=True)
            p.op("vector", "tensor_tensor", ternT_ch(ic), wqd[0][:], wqd[1][:],
                 ALU.subtract, drain=True, inc=("twV", 1))
        p.barrier(("tensor", "vector", "sync"))
        dump_chunks("tern", ternT_ch, width=1024)

        for oc in range(NC):
            for qg in range(4):
                bank = psbank((oc * 4 + qg) % 8)
                for ic in range(NC):
                    w = []
                    if ic == 0 and oc * 4 + qg >= 8:
                        w.append(("ffE", p.val("ffE") - 7))
                    p.op("tensor", "matmul", bank,
                         ternT_ch(ic)[:, 128 * oc:128 * (oc + 1)],
                         khat_ch(ic)[:, 512 * qg:512 * (qg + 1)],
                         start=(ic == 0), stop=(ic == NC - 1), waits=w,
                         inc=("ffT5", 1) if ic == NC - 1 else None)
                p.op("vector", "scalar_tensor_tensor",
                     xc(oc)[:, 512 * qg:512 * (qg + 1)], bank,
                     so_cols[:, oc:oc + 1], xc(oc)[:, 512 * qg:512 * (qg + 1)],
                     op0=ALU.mult, op1=ALU.add,
                     waits=[("ffT5", p.val("ffT5"))],
                     inc=("ffE", 1))
        p.barrier()
        dump_chunks("x_ffn", xc)
        if stage <= 2:
            _emit_out_zero(p, out_t, A2)
            p.finalize(block)
            return nc, dbg

        def xdq_ch(c):
            return A2b[:, 2048 * c:2048 * (c + 1)]

        u_scr = A2[:, 8192:10240]
        q8_scr = A2f8[:, 4 * 10240:4 * 10240 + 2048]
        rrow_sb = A2[0:1, 12800:13824]
        srow_sb = A2[0:1, 13824:14848]
        for c in range(NC):
            transpose_reduce([a1f(c)[:, 128 * tt:128 * tt + 128]
                              for tt in range(NTT)],
                             lambda wv: scl[:, 8 * wv:8 * wv + 8],
                             "qdT", "qdV",
                             first_waits=[("qdV2", p.val("qdV2"))])
            p.op("vector", "tensor_single_scalar", scl[:, 0:16], scl[:, 0:16],
                 448e-12, ALU.max, drain=True)
            p.op("vector", "reciprocal", scl[:, 16:32], scl[:, 0:16],
                 drain=True)
            p.op("vector", "tensor_single_scalar", scl[:, 16:32],
                 scl[:, 16:32], R224, ALU.mult, drain=True)
            p.op("vector", "tensor_single_scalar", scl[:, 32:48],
                 scl[:, 0:16], INV224, ALU.mult, inc=("qdV", 1))
            for half in range(2):
                cols = slice(1024 * half, 1024 * (half + 1))
                tv0 = p.val("qdV")
                rv0 = p.val("qdV2")
                for j in range(8):
                    w = []
                    if j == 0:
                        w.append(("qdV", tv0))
                        if half == 1:
                            w.append(("qdV2", rv0))
                    p.op("tensor", "transpose",
                         psA[0:1, 1024 + 128 * j:1152 + 128 * j],
                         scl[:, 16 + 8 * half + j:17 + 8 * half + j],
                         identf[:, :], waits=w)
                    p.op("tensor", "transpose",
                         psB[0:1, 1024 + 128 * j:1152 + 128 * j],
                         scl[:, 32 + 8 * half + j:33 + 8 * half + j],
                         identf[:, :], inc=("qdT2", 1) if j == 7 else None)
                p.op("vector", "tensor_copy", rrow_sb, psA[0:1, 1024:2048],
                     waits=[("qdT2", p.val("qdT2"))], drain=True)
                p.op("vector", "tensor_copy", srow_sb, psB[0:1, 1024:2048],
                     inc=("qdV2", 1))
                bc_row(psA[:, 0:1024], rrow_sb[0:1, :],
                       waits=[("qdV2", p.val("qdV2"))], inc=("qdT3", 1))
                p.op("vector", "tensor_tensor", u_scr[:, cols], a1f(c)[:, cols],
                     psA[:, 0:1024], ALU.mult,
                     waits=[("qdT3", p.val("qdT3"))], drain=True,
                     inc=("qdV3", 1))
                p.op("scalar", "activation", q8_scr[:, cols], u_scr[:, cols],
                     AF.Copy, waits=[("qdV3", p.val("qdV3"))], drain=True,
                     inc=("qdS", 1))
                bc_row(psB[:, 0:1024], srow_sb[0:1, :],
                       waits=[("qdV2", p.val("qdV2"))], inc=("qdT3", 1))
                p.op("vector", "tensor_tensor", xdq_ch(c)[:, cols],
                     q8_scr[:, cols], psB[:, 0:1024], ALU.mult,
                     waits=[("qdT3", p.val("qdT3")), ("qdS", p.val("qdS"))],
                     inc=("qdV", 1))
        p.barrier()
        dump_chunks("xdq", xdq_ch)
        if stage <= 3:
            _emit_out_zero(p, out_t, A2)
            p.finalize(block)
            return nc, dbg

        VA0 = 16384

        def vaug_stat(tt, h):
            base = VA0 + (tt * 8 + h) * 65
            return A2b[:, base:base + 65]

        def vaug_w2(tt, h0):
            base = VA0 + (tt * 8 + h0) * 65
            return A2b[:, base:base + 130].rearrange(
                "p (h c) -> p h c", h=2)[:, :, 0:64]

        SCf = 15424
        o_u = A2[:, SCf:SCf + 512]
        o_q8 = A2f8[:, 4 * 15936:4 * 15936 + 512]
        vt_scr = A2b[:, 2 * 14400:2 * 14400 + 2048]
        dqt_scr = A2b[:, 2 * 16064:2 * 16064 + 512]

        def qh_ap(j, off, n):
            return A1b[:, j * 2048 + off:j * 2048 + off + n]

        def kh_ap(j, off, n):
            return A1b[:, 8192 + j * 2048 + off:8192 + j * 2048 + off + n]

        def odq_ch(c):
            return A1b[:, 16384 + 2048 * c:16384 + 2048 * (c + 1)]

        def emit_wqd(src3, oc):
            par = oc % 2
            buf = wldf[par]
            qb = wqd[par]
            ws = "dWq" + str(par)
            w = [("wqV", p.val("wqV") - 1)] if p.val("wqLd") >= 2 else []
            p.op("sync", "dma_start", out=buf[:], in_=src3[oc], waits=w,
                 inc=(ws, 16))
            p.bump("wqLd")
            b3 = buf[:].rearrange("p (ic c) -> p ic c", c=128)
            q3 = qb[:].rearrange("p (ic c) -> p ic c", c=128)
            e3 = w8[:].rearrange("p (ic c) -> p ic c", c=128)
            p.op("vector", "reduce_max", scl[:, 32:40], b3, axis=AX.X,
                 apply_absolute_value=True, waits=[(ws, p.val(ws))],
                 drain=True)
            p.op("vector", "tensor_single_scalar", scl[:, 32:40], scl[:, 32:40],
                 448e-12, ALU.max, drain=True)
            p.op("vector", "reciprocal", scl[:, 40:48], scl[:, 32:40],
                 drain=True)
            p.op("vector", "tensor_single_scalar", scl[:, 40:48], scl[:, 40:48],
                 R224, ALU.mult, drain=True)
            p.op("vector", "tensor_single_scalar", scl[:, 48:56], scl[:, 32:40],
                 INV224, ALU.mult, drain=True)
            rbc = ap3(scl, 40, [[64, 128], [1, 8], [0, 128]])
            sbc = ap3(scl, 48, [[64, 128], [1, 8], [0, 128]])
            p.op("vector", "tensor_tensor", b3, b3, rbc, ALU.mult, drain=True,
                 inc=("wqV0", 1))
            p.op("scalar", "activation", w8[:], buf[:], AF.Copy,
                 waits=[("wqV0", p.val("wqV0"))], drain=True, inc=("wqS", 1))
            p.op("vector", "tensor_tensor", q3, e3, sbc, ALU.mult,
                 waits=[("wqS", p.val("wqS"))], drain=True, inc=("wqV", 1))
            return p.val("wqV")

        def emit_l0_half(h2):
            order = []
            for kind in ("q", "k"):
                base_oc = (0 if kind == "q" else 8) + 4 * h2
                for j in range(4):
                    order.append((kind, base_oc + j, j))
            vvs = emit_wqd(wqkv_r, order[0][1])
            ev_hist = []
            for i, (kind, oc, j) in enumerate(order):
                mme = []
                for qg in range(4):
                    bank = psbank(qg)
                    for ic in range(NC):
                        w = []
                        if ic == 0:
                            w.append(("wqV", vvs))
                            if i >= 1:
                                w.append(("qkE", ev_hist[i - 1][qg]))
                        p.op("tensor", "matmul", bank,
                             wqd[oc % 2][:, 128 * ic:128 * (ic + 1)],
                             xdq_ch(ic)[:, 512 * qg:512 * (qg + 1)],
                             start=(ic == 0), stop=(ic == NC - 1),
                             waits=w,
                             inc=("qkT", 1) if ic == NC - 1 else None)
                    mme.append(p.val("qkT"))
                if i + 1 < len(order):
                    vvs = emit_wqd(wqkv_r, order[i + 1][1])
                evs = []
                for qg in range(4):
                    dst = (qh_ap if kind == "q" else kh_ap)(
                        j, 512 * qg, 512)
                    w = [("qkT", mme[qg])]
                    if kind == "q":
                        p.op("vector", "tensor_scalar", dst, psbank(qg),
                             bqkv[:, oc:oc + 1], None, op0=ALU.add,
                             waits=w, inc=("qkE", 1))
                    else:
                        p.op("vector", "tensor_scalar", dst, psbank(qg),
                             bqkv[:, oc:oc + 1], 0.125, op0=ALU.add,
                             op1=ALU.mult,
                             waits=w, inc=("qkE", 1))
                    evs.append(p.val("qkE"))
                ev_hist.append(evs)
            p.barrier(("tensor", "vector", "scalar", "sync"))
            ones_ap = A2b[:, VA0 + 64:VA0 + 64 + 65 * 128].rearrange(
                "p (s c) -> p s c", c=65)[:, :, 0:1]
            p.op("vector", "memset", ones_ap, 1.0, drain=True)
            for j in range(4):
                oc = 16 + 4 * h2 + j
                vv = emit_wqd(wqkv_r, oc)
                mme = []
                for qg in range(4):
                    bank = psbank(qg)
                    for ic in range(NC):
                        w = [("wqV", vv)] if ic == 0 else []
                        if ic == 0 and qg == 0:
                            w.append(("vV", p.val("vV")))
                        if ic == 0 and qg == 0 and p.val("vE") >= 4:
                            w.append(("vE", p.val("vE") - 3))
                        p.op("tensor", "matmul", bank,
                             wqd[oc % 2][:, 128 * ic:128 * (ic + 1)],
                             xdq_ch(ic)[:, 512 * qg:512 * (qg + 1)],
                             start=(ic == 0), stop=(ic == NC - 1), waits=w,
                             inc=("qkT", 1) if ic == NC - 1 else None)
                    mme.append(p.val("qkT"))
                for qg in range(4):
                    p.op("vector", "tensor_scalar",
                         vt_scr[:, 512 * qg:512 * (qg + 1)], psbank(qg),
                         bqkv[:, oc:oc + 1], None, op0=ALU.add,
                         waits=[("qkT", mme[qg])],
                         inc=("vE", 1))
                transpose_tiles([(vt_scr[:, 128 * tt:128 * tt + 128],
                                  vaug_w2(tt, 2 * j)) for tt in range(NTT)],
                                identb[:, :], "vT", "vV",
                                first_waits=[("vE", p.val("vE"))])
            p.barrier(("tensor", "vector"))
            if h2 == 0 and "qT" in taps:
                p.barrier()
                for j in range(4):
                    p.op("gpsimd", "dma_start",
                         out=dbg["qT"][:, 2048 * j:2048 * (j + 1)],
                         in_=A1b[:, 2048 * j:2048 * (j + 1)],
                         waits=[("dDbg", p.val("dDbg"))], inc=("dDbg", 16))
                    p.op("gpsimd", "dma_start",
                         out=dbg["kT"][:, 2048 * j:2048 * (j + 1)],
                         in_=A1b[:, 8192 + 2048 * j:8192 + 2048 * (j + 1)],
                         waits=[("dDbg", p.val("dDbg"))], inc=("dDbg", 16))
                p.barrier()
            def sc_fn(s, idx, kc, ew):
                pr, qg = divmod(s, 4)
                par = idx % 2
                eps_ = psA[:, 1024 * par:1024 * par + 1024]
                p.op("tensor", "matmul", eps_[:, 0:512],
                     _kslice(A1b, pr, kc, 0), _qslice(A1b, pr, qg, 0),
                     start=True, stop=True,
                     waits=[("aS", ew)] if ew is not None else [],
                     tile_position=(0, 0))
                p.op("tensor", "matmul", eps_[:, 512:1024],
                     _kslice(A1b, pr, kc, 64), _qslice(A1b, pr, qg, 64),
                     start=True, stop=True, tile_position=(64, 0),
                     inc=("aT", 1))

            def eslot(idx):
                j = idx % 3
                return (ebuf[:, 1024 * j:1024 * (j + 1)] if j < 2
                        else A2b[:, 24704:25728])

            def ex_fn(s, idx, kc):
                par = idx % 2
                p.op("scalar", "activation", eslot(idx),
                     psA[:, 1024 * par:1024 * par + 1024],
                     AF.Exp, waits=[("aT", p.val("aT"))], inc=("aS", 1))

            def pv_fn(s, idx, kc, first, last, sv):
                pr, qg = divmod(s, 4)
                es = eslot(idx)
                base = 1024 * (s % 2)
                w0 = [sv]
                if first:
                    w0.append(("oV4", p.val("oV4")))
                p.op("tensor", "matmul", psB[0:65, base:base + 512],
                     vaug_stat(kc, 2 * pr), es[:, 0:512],
                     start=first, stop=last, waits=w0)
                p.op("tensor", "matmul", psB[0:65, base + 512:base + 1024],
                     vaug_stat(kc, 2 * pr + 1), es[:, 512:1024],
                     start=first, stop=last,
                     inc=("aT2", 1) if last else None)

            def parts_fn(s):
                pr, qg = divmod(s, 4)
                return _onorm_qd_parts(p, 4 * h2 + pr, qg, s % 2, psB, A2,
                                       A2f8, identf, identb, scl, odq_ch,
                                       o_u, o_q8, dqt_scr, ap3)

            _attn_pipeline(p, 16, lambda s: list(range(NTT)), sc_fn, ex_fn,
                           pv_fn, parts_fn)
            p.barrier()

        emit_l0_half(0)
        emit_l0_half(1)
        dump_chunks("odq", odq_ch)
        if "vaug" in taps:
            p.op("gpsimd", "dma_start", out=dbg["vaug"][:, :],
                 in_=A2b[:, VA0:VA0 + 8320],
                 waits=[("dDbg", p.val("dDbg"))], inc=("dDbg", 16))
            p.barrier()
        if stage <= 4:
            _emit_out_zero(p, out_t, A2)
            p.finalize(block)
            return nc, dbg

        vvs = emit_wqd(wout_r, 0)
        ev_hist = []
        for oc in range(NC):
            mme = []
            for qg in range(4):
                bank = psbank(qg)
                for ic in range(NC):
                    w = []
                    if ic == 0:
                        w.append(("wqV", vvs))
                        if oc >= 1:
                            w.append(("opE", ev_hist[oc - 1][qg]))
                    p.op("tensor", "matmul", bank,
                         wqd[oc % 2][:, 128 * ic:128 * (ic + 1)],
                         odq_ch(ic)[:, 512 * qg:512 * (qg + 1)],
                         start=(ic == 0), stop=(ic == NC - 1), waits=w,
                         inc=("qkT", 1) if ic == NC - 1 else None)
                mme.append(p.val("qkT"))
            if oc + 1 < NC:
                vvs = emit_wqd(wout_r, oc + 1)
            evs = []
            for qg in range(4):
                xl = xc(oc)[:, 512 * qg:512 * (qg + 1)]
                p.op("vector", "scalar_tensor_tensor", xl, psbank(qg),
                     bout[:, oc:oc + 1], xl, op0=ALU.add, op1=ALU.add,
                     waits=[("qkT", mme[qg])],
                     inc=("opE", 1))
                evs.append(p.val("opE"))
            ev_hist.append(evs)
        p.barrier()
        dump_chunks("x_l0", xc)
        if stage <= 5:
            _emit_out_zero(p, out_t, A2)
            p.finalize(block)
            return nc, dbg

        emit_layernorm(a1f, rows_off=0, sq_off=8192)
        p.barrier()
        dump_chunks("xn2", a1f)

        xnb_ch = lambda c: A2b[:, 2048 * c:2048 * (c + 1)]
        for c in range(NC):
            p.op("vector", "tensor_copy", xnb_ch(c), a1f(c), drain=(c == 0),
                 inc=("gV", 1) if c == NC - 1 else None)
        xlo_scr = wldf[0][:].bitcast(BF16)
        l_row = A2[0:1, 10240:12288]
        for ic in range(NC):
            p.op("vector", "tensor_tensor", xlo_scr[:, 0:2048],
                 a1f(ic), xnb_ch(ic), ALU.subtract, drain=(ic == 0),
                 waits=[("gT3", 4 * ic)] if ic > 0 else [],
                 inc=("gV", 1))
            gv = p.val("gV")
            for qg in range(4):
                for ps_i, (stat, mov) in enumerate(
                        ((wgh[:, ic:ic + 1], xnb_ch(ic)),
                         (wgl[:, ic:ic + 1], xnb_ch(ic)),
                         (wgh[:, ic:ic + 1], xlo_scr[:, 0:2048]))):
                    p.op("tensor", "matmul", psA[0:1, 512 * qg:512 * (qg + 1)],
                         stat, mov[:, 512 * qg:512 * (qg + 1)],
                         start=(ic == 0 and ps_i == 0),
                         stop=(ic == NC - 1 and ps_i == 2),
                         waits=[("gV", gv)] if ps_i == 0 else [],
                         inc=("gT3", 1) if ps_i == 2 else None)
        p.wait("vector", "gT3", p.val("gT3"))
        p.op("vector", "tensor_copy", l_row, psA[0:1, 0:2048], drain=True,
             inc=("gV", 1))
        if "lrow" in taps:
            p.barrier()
            p.op("gpsimd", "dma_start", out=dbg["lrow"][:, :], in_=l_row,
                 waits=[("dDbg", p.val("dDbg"))], inc=("dDbg", 16))
            p.wait("gpsimd", "dDbg", p.val("dDbg"))
            p.barrier()
        for tt in range(NTT):
            p.op("tensor", "transpose", psA[:, 1024 + tt:1025 + tt],
                 A2[0:1, 10240 + 128 * tt:10368 + 128 * tt], identf[0:1, 0:1],
                 waits=[("gV", p.val("gV"))] if tt == 0 else [],
                 inc=("gT", 1) if tt == NTT - 1 else None)
        p.op("vector", "tensor_copy", scl[:, 0:16], psA[:, 1024:1040],
             waits=[("gT", p.val("gT"))], drain=True, inc=("gV", 1))
        lbc = A2[:, 8192:10240]
        for half in range(2):
            bc_row(psA[:, 0:1024], l_row[0:1, 1024 * half:1024 * (half + 1)],
                   waits=[("gV", p.val("gV"))],
                   inc=("gT", 1))
            p.op("vector", "tensor_copy", lbc[:, 1024 * half:1024 * (half + 1)],
                 psA[:, 0:1024], waits=[("gT", p.val("gT"))], drain=True,
                 inc=("gV", 1))
        p.barrier(("tensor", "vector"))
        lo_c = scl[:, 48:49]
        hi_c = scl[:, 49:50]
        st_c = scl[:, 50:51]
        th_c = scl[:, 51:52]
        cnt_c = scl[:, 52:53]
        ok_c = scl[:, 53:54]
        p.op("vector", "memset", lo_c, -512.0, drain=True)
        p.op("vector", "memset", hi_c, 512.0, drain=True, inc=("gV", 1))
        for it in range(10):
            p.op("vector", "tensor_tensor", st_c, hi_c, lo_c, ALU.subtract,
                 waits=[("gT2", p.val("gT2"))], drain=True)
            p.op("vector", "tensor_single_scalar", st_c, st_c,
                 float(np.float32(1.0 / 128.0)), ALU.mult, drain=True)
            p.op("vector", "scalar_tensor_tensor", th_c, iotac[:, 0:1], st_c,
                 lo_c, op0=ALU.mult, op1=ALU.add, drain=True)
            p.op("vector", "tensor_scalar", psA[:, 0:2048], lbc, th_c, None,
                 op0=ALU.is_gt, op1=ALU.add, accum_out=cnt_c, drain=True)
            p.op("vector", "tensor_single_scalar", ok_c, cnt_c, 1024.0,
                 ALU.is_ge, drain=True, inc=("gV2", 1))
            p.op("tensor", "matmul", psB[0:1, 0:1], onescol[:, 0:1],
                 ok_c, start=True, stop=True,
                 waits=[("gV2", p.val("gV2"))], inc=("gT2", 1))
            p.op("vector", "tensor_copy", scl[0:1, 57:58], psB[0:1, 0:1],
                 waits=[("gT2", p.val("gT2"))], drain=True, inc=("gV2", 1))
            p.op("tensor", "matmul", psB[:, 1:2], onesrow[0:1, :],
                 scl[0:1, 57:58], start=True, stop=True,
                 waits=[("gV2", p.val("gV2"))], inc=("gT2", 1))
            p.op("vector", "scalar_tensor_tensor", th_c, psB[:, 1:2], -1.0,
                 st_c, op0=ALU.add, op1=ALU.mult,
                 waits=[("gT2", p.val("gT2"))], drain=True)
            p.op("vector", "tensor_add", lo_c, th_c, lo_c, drain=True)
            p.op("vector", "tensor_add", hi_c, lo_c, st_c, drain=True,
                 inc=("gV3", 1))
        p.barrier(("tensor", "vector"))
        if "locol" in taps:
            p.barrier()
            p.op("gpsimd", "dma_start", out=dbg["locol"][:, :], in_=lo_c,
                 waits=[("dDbg", p.val("dDbg"))], inc=("dDbg", 16))
            p.wait("gpsimd", "dDbg", p.val("dDbg"))
            p.barrier()
        sel_f = scl[:, 16:32]
        p.op("vector", "tensor_scalar", sel_f, scl[:, 0:16], lo_c, None,
             op0=ALU.is_gt, drain=True)
        sel_b = sclb[:, 112:128]
        p.op("vector", "tensor_copy", sel_b, sel_f, drain=True, inc=("gV", 1))
        p.op("tensor", "matmul", psB[:, 0:16], tri_b[:, :], sel_b,
             start=True, stop=True, waits=[("gV", p.val("gV"))],
             inc=("gT", 1))
        p.op("tensor", "matmul", psB[0:1, 16:32], onescol[:, 0:1], sel_f,
             start=True, stop=True, waits=[("gV", p.val("gV"))],
             inc=("gT", 1))
        p.wait("vector", "gT", p.val("gT"))
        p.op("vector", "tensor_copy", A2[0:1, 14400:14416], psB[0:1, 16:32],
             drain=True, inc=("gV", 1))
        p.barrier(("vector", "sync"))
        p.op("sync", "dma_start", out=scl[0:16, 56:57],
             in_=A2[0:1, 14400:14416], inc=("dF", 16))
        p.wait("tensor", "dF", p.val("dF"))
        p.op("tensor", "matmul", psB[0:16, 16:17], tri_f[0:16, 0:16],
             scl[0:16, 56:57], start=True, stop=True, inc=("gT", 1))
        p.wait("vector", "gT", p.val("gT"))
        p.op("vector", "tensor_copy", scl[0:16, 57:58], psB[0:16, 16:17],
             drain=True, inc=("gV", 1))
        p.op("tensor", "transpose", psB[0:1, 17:33], scl[0:16, 57:58],
             tri_f[0:16, 0:16] if False else identf[0:16, 0:16],
             waits=[("gV", p.val("gV"))], inc=("gT", 1))
        p.wait("vector", "gT", p.val("gT"))
        p.op("vector", "tensor_copy", A2[0:1, 14416:14432], psB[0:1, 17:33],
             drain=True, inc=("gV", 1))
        bc_row(psB[:, 32:48], A2[0:1, 14416:14432],
               waits=[("gV", p.val("gV"))], inc=("gT", 1))
        p.wait("vector", "gT", p.val("gT"))
        p.op("vector", "tensor_copy", scl[:, 16:32], psB[:, 0:16], drain=True)
        p.op("vector", "tensor_tensor", scl[:, 16:32], scl[:, 16:32],
             psB[:, 32:48], ALU.add, drain=True)
        p.op("vector", "tensor_single_scalar", scl[:, 32:48], scl[:, 16:32],
             4096.0, ALU.add, drain=True)
        p.op("vector", "tensor_scalar", sel_f, scl[:, 0:16], lo_c, None,
             op0=ALU.is_gt, drain=True)
        p.op("vector", "scalar_tensor_tensor", scl[:, 32:48], sel_f, -4096.0,
             scl[:, 32:48], op0=ALU.mult, op1=ALU.add, drain=True,
             inc=("gV", 1))
        if "rankp" in taps:
            p.barrier()
            p.op("gpsimd", "dma_start", out=dbg["rankp"][:, :],
                 in_=scl[:, 32:48], waits=[("dDbg", p.val("dDbg"))],
                 inc=("dDbg", 16))
            p.wait("gpsimd", "dDbg", p.val("dDbg"))
            p.barrier()
        lcol_b = sclb[:, 96:112]
        p.op("vector", "tensor_copy", lcol_b, scl[:, 0:16], drain=True,
             inc=("gV", 1))
        p.barrier()

        pairs2 = []
        for tt in range(NTT):
            for fc in range(NC):
                src2 = xnb_ch(fc)[:, 128 * tt:128 * tt + 128]
                dst2 = A2b[:, 16384 + 1024 * tt + 128 * fc:
                           16384 + 1024 * tt + 128 * fc + 128]
                pairs2.append((src2, dst2))
        psBb2 = psB[:].bitcast(BF16)
        transpose_tiles(pairs2, identb[:, :], "tT", "tV")
        p.barrier()

        def attn_in_ch(fc):
            return A1b[:, 1024 * fc:1024 * (fc + 1)]

        bi = 0
        for sg in range(2):
            for tt in range(NTT):
                eb = ebuf[:, 512 * (bi % 2):512 * (bi % 2) + 512]
                w = [("gaT", p.val("gaT") - 1)] if bi >= 2 else []
                p.op("vector", "tensor_scalar", eb,
                     iotar[:, 512 * sg:512 * (sg + 1)], scl[:, 32 + tt:33 + tt],
                     None, op0=ALU.is_equal,
                     waits=w, drain=(bi == 0), inc=("gaV", 1))
                gav = p.val("gaV")
                for fc in range(NC):
                    p.op("tensor", "matmul", psbank(fc),
                         A2b[:, 16384 + 1024 * tt + 128 * fc:
                             16384 + 1024 * tt + 128 * fc + 128],
                         eb, start=(tt == 0), stop=(tt == NTT - 1),
                         waits=[("gaV", gav)] if fc == 0 else [],
                         inc=("gaT", 1) if fc == NC - 1 else None)
                bi += 1
            gtv = p.val("gaT")
            for hh in range(2):
                dst = bass.AP(A1b.tensor, 4096 * hh + 512 * sg,
                              [[32768, 128], [1024, 4], [1, 512]])
                p.op("vector", "tensor_copy", dst,
                     (psA if hh == 0 else psB)[:, 0:2048].rearrange(
                         "p (f t) -> p f t", f=4),
                     waits=[("gaT", gtv)] if hh == 0 else [],
                     inc=("gaV", 1) if hh == 1 else None)
        for sg in range(2):
            for tt in range(NTT):
                eb = ebuf[:, 512 * (bi % 2):512 * (bi % 2) + 512]
                w = [("gaT", p.val("gaT") - 1)] if bi >= 2 else []
                p.op("vector", "tensor_scalar", eb,
                     iotar[:, 512 * sg:512 * (sg + 1)], scl[:, 32 + tt:33 + tt],
                     None, op0=ALU.is_equal,
                     waits=w, inc=("gaV", 1))
                gav = p.val("gaV")
                p.op("tensor", "matmul", psA[0:1, 1024:1536],
                     lcol_b[:, tt:tt + 1], eb,
                     start=(tt == 0), stop=(tt == NTT - 1),
                     waits=[("gaV", gav)], inc=("gaT", 1))
                bi += 1
            p.op("vector", "tensor_copy",
                 A2[0:1, 14400 + 512 * sg:14400 + 512 * (sg + 1)],
                 psA[0:1, 1024:1536], waits=[("gaT", p.val("gaT"))],
                 drain=True, inc=("gaV", 1))
        p.barrier(("tensor", "vector", "scalar"))
        if "attn_in" in taps:
            dump_chunks("attn_in", attn_in_ch, width=1024)
        if "lsel" in taps:
            p.barrier()
            p.op("gpsimd", "dma_start", out=dbg["lsel"][:, :],
                 in_=A2[0:1, 14400:15424],
                 waits=[("dDbg", p.val("dDbg"))], inc=("dDbg", 16))
            p.wait("gpsimd", "dDbg", p.val("dDbg"))
            p.barrier()
        lsel_row = A2[0:1, 14400:15424]
        p.op("scalar", "activation", lsel_row, lsel_row, AF.Exp, scale=-1.0,
             drain=True, inc=("gS", 1))
        p.op("vector", "tensor_single_scalar", lsel_row, lsel_row, 1.0,
             ALU.add, waits=[("gS", p.val("gS"))], drain=True)
        p.op("vector", "reciprocal", lsel_row, lsel_row, drain=True,
             inc=("gV", 1))
        for cc in range(8):
            p.op("tensor", "transpose", psA[:, 1024 + cc:1025 + cc],
                 A2[0:1, 14400 + 128 * cc:14528 + 128 * cc], identf[0:1, 0:1],
                 waits=[("gV", p.val("gV"))] if cc == 0 else [],
                 inc=("gT", 1) if cc == 7 else None)
        p.op("vector", "tensor_copy", scl[:, 48:56], psA[:, 1024:1032],
             waits=[("gT", p.val("gT"))], drain=True, inc=("gV", 1))
        if "gcols" in taps:
            p.barrier()
            p.op("gpsimd", "dma_start", out=dbg["gcols"][:, :],
                 in_=scl[:, 48:56], waits=[("dDbg", p.val("dDbg"))],
                 inc=("dDbg", 16))
            p.wait("gpsimd", "dDbg", p.val("dDbg"))
            p.barrier()

        def qf_ap(j, off, n):
            return A1b[:, 8192 + j * 1024 + off:8192 + j * 1024 + off + n]

        def kf_ap(j, off, n):
            return A1b[:, 16384 + j * 1024 + off:16384 + j * 1024 + off + n]

        def of_ch(fc):
            return A1b[:, 24576 + 1024 * fc:24576 + 1024 * (fc + 1)]

        VAF = 16384

        def vaugf_stat(sc, h):
            base = VAF + (sc * 16 + h) * 65
            return A2b[:, base:base + 65]

        def vaugf_w2(sc, h0):
            base = VAF + (sc * 16 + h0) * 65
            return A2b[:, base:base + 130].rearrange(
                "p (h c) -> p h c", h=2)[:, :, 0:64]

        ones_ap2 = A2b[:, VAF + 64:VAF + 64 + 65 * 128].rearrange(
            "p (s c) -> p s c", c=65)[:, :, 0:1]
        p.op("vector", "memset", ones_ap2, 1.0, drain=True)
        vt2 = ebuf[:, 1024:2048]
        fevh = []
        for kind, wsrc in (("q", fw_r["Wq"]), ("k", fw_r["Wk"]),
                           ("v", fw_r["Wv"])):
            for oc in range(NC):
                par = oc % 2
                ws = "dL" + str(par)
                p.op("gpsimd", "dma_start", out=wqd[par][:], in_=wsrc[oc],
                     waits=[("fpT", max(0, p.val("fpT") - 2))],
                     inc=(ws, 16))
                wvv = p.val(ws)
                qgs = 2
                mme = []
                for qg in range(qgs):
                    bank = psbank(qg)
                    for ic in range(NC):
                        w = [(ws, wvv)] if ic == 0 else []
                        if ic == 0 and fevh:
                            w.append(("fpV", fevh[-1][qg]))
                        p.op("tensor", "matmul", bank,
                             wqd[par][:, 128 * ic:128 * (ic + 1)],
                             attn_in_ch(ic)[:, 512 * qg:512 * (qg + 1)],
                             start=(ic == 0), stop=(ic == NC - 1), waits=w,
                             inc=("fpT", 1) if ic == NC - 1 else None)
                    mme.append(p.val("fpT"))
                evs = []
                if kind == "q":
                    for qg in range(qgs):
                        p.op("vector", "tensor_copy",
                             qf_ap(oc, 512 * qg, 512),
                             psbank(qg),
                             waits=[("fpT", mme[qg])],
                             inc=("fpV", 1))
                        evs.append(p.val("fpV"))
                elif kind == "k":
                    for qg in range(qgs):
                        p.op("vector", "tensor_single_scalar",
                             kf_ap(oc, 512 * qg, 512),
                             psbank(qg), 0.125, ALU.mult,
                             waits=[("fpT", mme[qg])],
                             inc=("fpV", 1))
                        evs.append(p.val("fpV"))
                else:
                    for qg in range(qgs):
                        p.op("vector", "tensor_copy",
                             vt2[:, 512 * qg:512 * (qg + 1)], psbank(qg),
                             waits=[("fpT", mme[qg])],
                             inc=("fpV", 1))
                        evs.append(p.val("fpV"))
                    transpose_tiles([(vt2[:, 128 * st2:128 * st2 + 128],
                                      vaugf_w2(st2, 2 * (oc % 8)))
                                     for st2 in range(8)],
                                    identb[:, :], "fvT", "fvV",
                                    first_waits=[("fpV", p.val("fpV"))])
                fevh.append(evs)
        p.barrier()

        def fsc_fn(s, idx, kc, ew):
            pr, qg = divmod(s, 2)
            par = idx % 2
            eps_ = psA[:, 1024 * par:1024 * par + 1024]
            kt = A1b[:, 16384 + pr * 1024 + 128 * kc:
                     16384 + pr * 1024 + 128 * kc + 128]
            qt = A1b[:, 8192 + pr * 1024 + 512 * qg:
                     8192 + pr * 1024 + 512 * qg + 512]
            p.op("tensor", "matmul", eps_[:, 0:512], kt[0:64, :], qt[0:64, :],
                 start=True, stop=True,
                 waits=[("aS", ew)] if ew is not None else [],
                 tile_position=(0, 0))
            p.op("tensor", "matmul", eps_[:, 512:1024], kt[64:128, :],
                 qt[64:128, :], start=True, stop=True, tile_position=(64, 0),
                 inc=("aT", 1))

        def feslot(idx):
            j = idx % 3
            return (ebuf[:, 1024 * j:1024 * (j + 1)] if j < 2
                    else A2b[:, 24704:25728])

        def fex_fn(s, idx, kc):
            par = idx % 2
            p.op("scalar", "activation", feslot(idx),
                 psA[:, 1024 * par:1024 * par + 1024], AF.Exp,
                 waits=[("aT", p.val("aT"))], inc=("aS", 1))

        def fpv_fn(s, idx, kc, first, last, sv):
            pr, qg = divmod(s, 2)
            es = feslot(idx)
            base = 1024 * (s % 2)
            w0 = [sv]
            if first:
                w0.append(("oV4", p.val("oV4")))
            p.op("tensor", "matmul", psB[0:65, base:base + 512],
                 vaugf_stat(kc, 2 * pr), es[:, 0:512],
                 start=first, stop=last, waits=w0)
            p.op("tensor", "matmul", psB[0:65, base + 512:base + 1024],
                 vaugf_stat(kc, 2 * pr + 1), es[:, 512:1024],
                 start=first, stop=last,
                 inc=("aT2", 1) if last else None)

        def fparts_fn(s):
            pr, qg = divmod(s, 2)
            return _onorm_plain_parts(p, pr, qg, s % 2, psB, A2, A2b, identf,
                                      identb, scl, of_ch, ap3)

        _attn_pipeline(p, 16, lambda s: list(range(8)), fsc_fn, fex_fn,
                       fpv_fn, fparts_fn)
        p.barrier()

        def y2_ch(sc):
            return A1b[:, 1024 * sc:1024 * (sc + 1)]

        for og in range(2):
            for fc in range(NC):
                par = fc % 2
                ws = "dL" + str(par)
                p.op("gpsimd", "dma_start", out=wqd[par][:],
                     in_=fWo_nat[128 * fc:128 * (fc + 1), :],
                     waits=[("y2T", max(0, p.val("y2T") - 1))], inc=(ws, 16))
                wvv = p.val(ws)
                for sc in range(NC):
                    w = [(ws, wvv)] if sc == 0 else []
                    if fc == 0 and sc == 0:
                        w.append(("y2V", p.val("y2V")))
                    p.op("tensor", "matmul", psbank(sc),
                         of_ch(fc)[:, 128 * sc:128 * (sc + 1)],
                         wqd[par][:, 512 * og:512 * (og + 1)],
                         start=(fc == 0), stop=(fc == NC - 1), waits=w,
                         inc=("y2T", 1) if sc == NC - 1 else None)
            for sc in range(NC):
                p.op("vector", "tensor_scalar", y2_ch(sc)[:, 512 * og:512 * (og + 1)],
                     psbank(sc), scl[:, 48 + sc:49 + sc], None, op0=ALU.mult,
                     waits=[("y2T", p.val("y2T"))] if sc == 0 else [],
                     drain=(sc == 0), inc=("y2V", 1))
        p.barrier()
        if "y2g" in taps:
            dump_chunks("y2g", y2_ch, width=1024)

        for j in range(16):
            pst = psA if j < 8 else psB
            p.op("tensor", "transpose",
                 pst[0:1, 1024 + 128 * (j % 8):1152 + 128 * (j % 8)],
                 scl[:, 32 + j:33 + j], identf[:, :],
                 waits=[("scV", p.val("scV"))] if j == 0 else [],
                 inc=("scT", 1) if j == 15 else None)
        p.op("vector", "tensor_copy", A2[0:1, 12352:13376],
             psA[0:1, 1024:2048],
             waits=[("scT", p.val("scT"))], drain=True)
        p.op("vector", "tensor_copy", A2[0:1, 13376:14400],
             psB[0:1, 1024:2048], inc=("scV", 1))
        rankbc = A2[:, 12352:14400]
        for half in range(2):
            bc_row(psA[:, 0:1024],
                   A2[0:1, 12352 + 1024 * half:12352 + 1024 * (half + 1)],
                   waits=[("scV", p.val("scV"))], inc=("scT", 1))
            p.op("vector", "tensor_copy",
                 rankbc[:, 1024 * half:1024 * (half + 1)], psA[:, 0:1024],
                 waits=[("scT", p.val("scT"))], drain=True, inc=("scV", 1))
        p.barrier(("tensor", "vector"))
        for sc in range(NC):
            p.op("vector", "tensor_single_scalar", scl[:, 16 + sc:17 + sc],
                 iotac[:, 0:1], float(128 * sc), ALU.add, drain=(sc == 0))
        bi2 = 0
        for tg in range(4):
            for sc in range(NC):
                eb = ebuf[:, 512 * (bi2 % 2):512 * (bi2 % 2) + 512]
                w = [("scT", p.val("scT") - 1)] if bi2 >= 2 else []
                p.op("vector", "tensor_scalar", eb,
                     rankbc[:, 512 * tg:512 * (tg + 1)],
                     scl[:, 16 + sc:17 + sc], None,
                     op0=ALU.is_equal, waits=w, drain=(bi2 == 0),
                     inc=("scV", 1))
                scv = p.val("scV")
                for fc in range(NC):
                    p.op("tensor", "matmul", psbank(fc),
                         y2_ch(sc)[:, 128 * fc:128 * (fc + 1)],
                         eb, start=(sc == 0), stop=(sc == NC - 1),
                         waits=[("scV", scv)] if fc == 0 else [],
                         inc=("scT", 1) if fc == NC - 1 else None)
                bi2 += 1
            stv = p.val("scT")
            for hh in range(2):
                dst = bass.AP(AXr[0:128, 0:1].tensor, 8192 * hh + 512 * tg,
                              [[16384, 128], [2048, 4], [1, 512]])
                p.op("vector", "tensor_tensor", dst, dst,
                     (psA if hh == 0 else psB)[:, 0:2048].rearrange(
                         "p (f t) -> p f t", f=4), ALU.add,
                     waits=[("scT", stv)] if hh == 0 else [],
                     inc=("scV", 1) if hh == 1 else None)
        p.barrier()
        dump_chunks("x_ffn2", xc)
        if stage <= 7:
            _emit_out_zero(p, out_t, A2)
            p.finalize(block)
            return nc, dbg

        lmask_sb = A2b[:, 24704:28800]
        p.op("gpsimd", "dma_start", out=lmask_sb, in_=c_lmasks[:, :],
             inc=("dInG", 16))
        p.wait("vector", "dInG", p.val("dInG"))

        def q2_ap(j, off, n):
            return A1b[:, j * 2048 + off:j * 2048 + off + n]

        def k2_ap(j, off, n):
            return A1b[:, 8192 + j * 2048 + off:8192 + j * 2048 + off + n]

        def o2_ch(c):
            return A1b[:, 16384 + 2048 * c:16384 + 2048 * (c + 1)]

        VL = 16384

        def vaug2_stat(tt, h):
            base = VL + (tt * 8 + h) * 65
            return A2b[:, base:base + 65]

        def vaug2_w2(tt, h0):
            base = VL + (tt * 8 + h0) * 65
            return A2b[:, base:base + 130].rearrange(
                "p (h c) -> p h c", h=2)[:, :, 0:64]

        vt3 = A2b[:, 28800:30848]

        def emit_l1_half(h2):
            ones_ap3 = A2b[:, VL + 64:VL + 64 + 65 * 128].rearrange(
                "p (s c) -> p s c", c=65)[:, :, 0:1]
            p.op("vector", "memset", ones_ap3, 1.0, drain=True,
                 inc=("l1V", 1))
            evh = []
            for kind, wsrc, bias in (("q", l1w_r["wq"], bcol["q"]),
                                     ("k", l1w_r["wk"], bcol["k"]),
                                     ("v", l1w_r["wv"], bcol["v"])):
                for j in range(4):
                    oc = 4 * h2 + j
                    par = oc % 2
                    ws = "dL" + str(par)
                    p.op("gpsimd", "dma_start", out=wqd[par][:], in_=wsrc[oc],
                         waits=[("l1T", max(0, p.val("l1T") - 4))],
                         inc=(ws, 16))
                    wvv = p.val(ws)
                    mme = []
                    for qg in range(4):
                        bank = psbank(qg)
                        for ic in range(NC):
                            w = [(ws, wvv)] if ic == 0 else []
                            if ic == 0 and evh:
                                w.append(("l1E", evh[-1][qg]))
                            if ic == 0 and qg == 0:
                                w.append(("l1vV", p.val("l1vV")))
                            p.op("tensor", "matmul", bank,
                                 wqd[par][:, 128 * ic:128 * (ic + 1)],
                                 xnb_ch(ic)[:, 512 * qg:512 * (qg + 1)],
                                 start=(ic == 0), stop=(ic == NC - 1), waits=w,
                                 inc=("l1T", 1) if ic == NC - 1 else None)
                        mme.append(p.val("l1T"))
                    evs = []
                    if kind == "q":
                        for qg in range(4):
                            p.op("vector", "tensor_scalar",
                                 q2_ap(j, 512 * qg, 512), psbank(qg),
                                 bias[:, oc:oc + 1], None, op0=ALU.add,
                                 waits=[("l1T", mme[qg])],
                                 inc=("l1E", 1))
                            evs.append(p.val("l1E"))
                    elif kind == "k":
                        for qg in range(4):
                            p.op("vector", "tensor_scalar",
                                 k2_ap(j, 512 * qg, 512), psbank(qg),
                                 bias[:, oc:oc + 1], 0.125, op0=ALU.add,
                                 op1=ALU.mult,
                                 waits=[("l1T", mme[qg])],
                                 inc=("l1E", 1))
                            evs.append(p.val("l1E"))
                    else:
                        for qg in range(4):
                            p.op("vector", "tensor_scalar",
                                 vt3[:, 512 * qg:512 * (qg + 1)], psbank(qg),
                                 bias[:, oc:oc + 1], None, op0=ALU.add,
                                 waits=[("l1T", mme[qg])],
                                 inc=("l1E", 1))
                            evs.append(p.val("l1E"))
                        transpose_tiles([(vt3[:, 128 * tt:128 * tt + 128],
                                          vaug2_w2(tt, 2 * j))
                                         for tt in range(NTT)],
                                        identb[:, :], "l1vT", "l1vV",
                                        first_waits=[("l1E",
                                                      p.val("l1E"))])
                    evh.append(evs)
            p.barrier(("tensor", "vector"))
            def lkcs_fn(s):
                qg = s % 4
                return [kc for kc in range(4 * qg - 2, 4 * qg + 6)
                        if 0 <= kc < 16]

            def lsc_fn(s, idx, kc, ew):
                pr, qg = divmod(s, 4)
                par = idx % 2
                eps_ = psA[:, 1024 * par:1024 * par + 1024]
                kt = A1b[:, 8192 + pr * 2048 + 128 * kc:
                         8192 + pr * 2048 + 128 * kc + 128]
                qt = A1b[:, pr * 2048 + 512 * qg:pr * 2048 + 512 * qg + 512]
                p.op("tensor", "matmul", eps_[:, 0:512], kt[0:64, :],
                     qt[0:64, :], start=True, stop=True,
                     waits=[("aS", ew)] if ew is not None else [],
                     tile_position=(0, 0))
                p.op("tensor", "matmul", eps_[:, 512:1024], kt[64:128, :],
                     qt[64:128, :], start=True, stop=True,
                     tile_position=(64, 0), inc=("aT", 1))

            def leslot(idx):
                j = idx % 3
                return (ebuf[:, 1024 * j:1024 * (j + 1)] if j < 2
                        else A2b[:, 31360:32384])

            def lex_fn(s, idx, kc):
                par = idx % 2
                p.op("scalar", "activation", leslot(idx),
                     psA[:, 1024 * par:1024 * par + 1024], AF.Exp,
                     waits=[("aT", p.val("aT"))], inc=("aS", 1))

            def lmk_fn(s, idx, kc):
                pr, qg = divmod(s, 4)
                es = leslot(idx)
                mi = kc - 4 * qg + 2
                lm = lmask_sb[:, 512 * mi:512 * (mi + 1)]
                eng = "gpsimd" if idx % 2 == 0 else "vector"
                sem = "aM" if idx % 2 == 0 else "aMv"
                p.op(eng, "tensor_tensor", es[:, 0:512], es[:, 0:512],
                     lm, ALU.mult, waits=[("aS", p.val("aS"))])
                p.op(eng, "tensor_tensor", es[:, 512:1024],
                     es[:, 512:1024], lm, ALU.mult, inc=(sem, 1))
                return (sem, p.val(sem))

            def lpv_fn(s, idx, kc, first, last, sv):
                pr, qg = divmod(s, 4)
                es = leslot(idx)
                base = 1024 * (s % 2)
                w0 = [sv]
                if first:
                    w0.append(("oV4", p.val("oV4")))
                p.op("tensor", "matmul", psB[0:65, base:base + 512],
                     vaug2_stat(kc, 2 * pr), es[:, 0:512],
                     start=first, stop=last, waits=w0)
                p.op("tensor", "matmul", psB[0:65, base + 512:base + 1024],
                     vaug2_stat(kc, 2 * pr + 1), es[:, 512:1024],
                     start=first, stop=last,
                     inc=("aT2", 1) if last else None)

            def lparts_fn(s):
                pr, qg = divmod(s, 4)
                return _onorm_plain_parts(p, 4 * h2 + pr, qg, s % 2, psB, A2,
                                          A2b, identf, identb, scl, o2_ch,
                                          ap3)

            _attn_pipeline(p, 16, lkcs_fn, lsc_fn, lex_fn, lpv_fn, lparts_fn,
                           emit_mask=lmk_fn)
            p.barrier()

        emit_l1_half(0)
        emit_l1_half(1)
        dump_chunks("o2", o2_ch, n=8, width=2048)

        oevh = []
        for oc in range(NC):
            par = oc % 2
            ws = "dL" + str(par)
            p.op("gpsimd", "dma_start", out=wqd[par][:], in_=l1w_r["wo"][oc],
                 waits=[("opT", max(0, p.val("opT") - 4))], inc=(ws, 16))
            wvv = p.val(ws)
            mme = []
            for qg in range(4):
                bank = psbank(qg)
                for ic in range(NC):
                    w = [(ws, wvv)] if ic == 0 else []
                    if ic == 0 and oevh:
                        w.append(("opV", oevh[-1][qg]))
                    p.op("tensor", "matmul", bank,
                         wqd[par][:, 128 * ic:128 * (ic + 1)],
                         o2_ch(ic)[:, 512 * qg:512 * (qg + 1)],
                         start=(ic == 0), stop=(ic == NC - 1), waits=w,
                         inc=("opT", 1) if ic == NC - 1 else None)
                mme.append(p.val("opT"))
            evs = []
            for qg in range(4):
                xl = xc(oc)[:, 512 * qg:512 * (qg + 1)]
                p.op("vector", "scalar_tensor_tensor", xl, psbank(qg),
                     bcol["o"][:, oc:oc + 1], xl, op0=ALU.add, op1=ALU.add,
                     waits=[("opT", mme[qg])],
                     inc=("opV", 1))
                evs.append(p.val("opV"))
            oevh.append(evs)
        p.barrier()
        dump_chunks("x_l1", xc)

        for tt in range(NTT):
            buf = tt % 2
            dsem = "dO" + str(buf)
            pst = psA if buf == 0 else psB
            w = [("foV", p.val("foV") - 1)] if tt >= 2 else []
            for fc in range(NC):
                p.op("tensor", "transpose",
                     pst[:, 1024 + 128 * fc:1024 + 128 * fc + 128],
                     xc(fc)[:, 128 * tt:128 * tt + 128], identf[:, :],
                     waits=w if fc == 0 else [],
                     inc=("foT", 1) if fc == NC - 1 else None)
            ev = [("foT", p.val("foT"))]
            if tt >= 2:
                ev.append((dsem, 16 * (tt // 2)))
            p.op("vector", "tensor_copy", A2[:, 1024 * buf:1024 * buf + 1024],
                 pst[:, 1024:2048], waits=ev, inc=("foV", 1))
            p.op("sync", "dma_start", out=out_t[128 * tt:128 * (tt + 1), :],
                 in_=A2[:, 1024 * buf:1024 * buf + 1024],
                 waits=[("foV", p.val("foV"))], inc=(dsem, 16))
        p.wait("sync", "dO0", p.val("dO0"))
        p.wait("sync", "dO1", p.val("dO1"))
        p.barrier()
        p.finalize(block)
        return nc, dbg


def _attn_pipeline(p, nslots, kcs_fn, emit_scores, emit_exp, emit_pv,
                   parts_fn, emit_mask=None):
    gesv = []
    prev = None
    for s in range(nslots):
        parts = prev
        if parts is not None:
            parts[0]()
        kcs = kcs_fn(s)
        n = len(kcs)
        pvv = []
        called = set()
        for idx, kc in enumerate(kcs):
            ew = gesv[-2] if len(gesv) >= 2 else None
            emit_scores(s, idx, kc, ew)
            emit_exp(s, idx, kc)
            gesv.append(p.val("aS"))
            if emit_mask is not None:
                pvv.append(emit_mask(s, idx, kc))
            else:
                pvv.append(("aS", gesv[-1]))
            if parts is not None:
                if idx == 2 and 1 not in called:
                    parts[1]()
                    called.add(1)
                if idx == 3 and 2 not in called:
                    parts[2]()
                    called.add(2)
                if idx == 6 and 3 not in called:
                    parts[3]()
                    called.add(3)
            if idx >= 2:
                emit_pv(s, idx - 2, kcs[idx - 2], idx - 2 == 0, False,
                        pvv[idx - 2])
        if parts is not None:
            for i in (1, 2, 3):
                if i not in called:
                    parts[i]()
        emit_pv(s, n - 2, kcs[n - 2], n - 2 == 0, False, pvv[n - 2])
        emit_pv(s, n - 1, kcs[n - 1], False, True, pvv[n - 1])
        if parts is not None:
            parts[4]()
        prev = parts_fn(s)
    for f in prev:
        f()


def _kslice(A1b, pr, kc, row0):
    apx = A1b[:, 8192 + pr * 2048 + 128 * kc: 8192 + pr * 2048 + 128 * kc + 128]
    return apx[row0:row0 + 64, :]


def _qslice(A1b, pr, qg, row0):
    apx = A1b[:, pr * 2048 + 512 * qg: pr * 2048 + 512 * qg + 512]
    return apx[row0:row0 + 64, :]


def _onorm_qd_parts(p, ch, qg, pi, psB, A2, A2f8, identf, identb, scl,
                    odq_ch, o_u, o_q8, dqt_scr, ap3):
    ALUx = ALU
    base = 1024 * pi
    bb = 2048 * pi
    stg8 = A2[:, 14400:15424]
    psBb = psB[:].bitcast(BF16)

    def staging():
        for hh in range(2):
            p.op("vector", "tensor_copy", stg8[0:65, 512 * hh:512 * hh + 512],
                 psB[0:65, base + 512 * hh:base + 512 * hh + 512],
                 waits=[("aT2", p.val("aT2")), ("oT", p.val("oT"))]
                 if hh == 0 else [],
                 inc=("oV", 1) if hh == 1 else None)

    def fwdT():
        for t in range(4):
            for hh in range(2):
                i = 2 * t + hh
                off = base + 65 * i if i < 4 else base + 512 + 65 * (i - 4)
                p.op("tensor", "transpose", psB[:, off:off + 65],
                     stg8[0:65, 512 * hh + 128 * t:512 * hh + 128 * t + 128],
                     identf[0:65, 0:65],
                     waits=[("oV", p.val("oV"))] if i == 0 else [],
                     inc=("oT", 1) if i == 7 else None)

    def chain1():
        p.wait("vector", "oT", p.val("oT"))
        p.op("vector", "reciprocal", scl[:, 16:20],
             ap3(psB, base + 64, [[2048, 128], [65, 4], [1, 1]]), drain=True)
        p.op("vector", "reciprocal", scl[:, 20:24],
             ap3(psB, base + 512 + 64, [[2048, 128], [65, 4], [1, 1]]))
        rzc0 = ap3(scl, 16, [[64, 128], [1, 4], [0, 64]])
        rzc1 = ap3(scl, 20, [[64, 128], [1, 4], [0, 64]])
        p.op("vector", "tensor_tensor",
             ap3(A2, 15424, [[16384, 128], [64, 4], [1, 64]]),
             ap3(psB, base, [[2048, 128], [65, 4], [1, 64]]), rzc0, ALUx.mult,
             drain=True)
        p.op("vector", "tensor_tensor",
             ap3(A2, 15424 + 256, [[16384, 128], [64, 4], [1, 64]]),
             ap3(psB, base + 512, [[2048, 128], [65, 4], [1, 64]]), rzc1,
             ALUx.mult)
        p.op("vector", "reduce_max", scl[:, 24:32], uv_sb(A2, ap3), axis=AX.X,
             apply_absolute_value=True, drain=True)
        p.op("vector", "reduce_max", scl[:, 16:20],
             ap3(scl, 24, [[64, 128], [2, 4], [1, 2]]), axis=AX.X, drain=True)
        p.op("vector", "tensor_single_scalar", scl[:, 16:20], scl[:, 16:20],
             448e-12, ALUx.max, drain=True)
        p.op("vector", "reciprocal", scl[:, 20:24], scl[:, 16:20], drain=True)
        p.op("vector", "tensor_single_scalar", scl[:, 20:24], scl[:, 20:24],
             R224, ALUx.mult, drain=True)
        p.op("vector", "tensor_single_scalar", scl[:, 24:28], scl[:, 16:20],
             INV224, ALUx.mult)
        rt = ap3(scl, 20, [[64, 128], [1, 4], [0, 128]])
        u4 = ap3(A2, 15424, [[16384, 128], [128, 4], [1, 128]])
        p.op("vector", "tensor_tensor", u4, u4, rt, ALUx.mult, drain=True,
             inc=("oV2", 1))

    def chain2():
        st_ = ap3(scl, 24, [[64, 128], [1, 4], [0, 128]])
        p.op("scalar", "activation", o_q8[:, 0:512], o_u[:, 0:512], AF.Copy,
             waits=[("oV2", p.val("oV2"))], drain=True, inc=("oS", 1))
        p.op("vector", "tensor_tensor",
             dqt_scr.rearrange("p (t c) -> p t c", t=4),
             o_q8[:, 0:512].rearrange("p (t c) -> p t c", t=4), st_,
             ALUx.mult,
             waits=[("oS", p.val("oS")), ("oT2", p.val("oT2"))],
             inc=("oV3", 1))

    def backT_evac():
        for t in range(4):
            off = bb + 128 * t if t < 2 else bb + 1024 + 128 * (t - 2)
            p.op("tensor", "transpose", psBb[:, off:off + 128],
                 dqt_scr[:, 128 * t:128 * t + 128], identb[:, :],
                 waits=[("oV3", p.val("oV3"))] if t == 0 else [],
                 inc=("oT2", 1) if t == 3 else None)
        otv = p.val("oT2")
        for hh in range(2):
            p.op("vector", "tensor_copy",
                 odq_ch(ch)[:, 512 * qg + 256 * hh:512 * qg + 256 * hh + 256],
                 psBb[:, bb + 1024 * hh:bb + 1024 * hh + 256],
                 waits=[("oT2", otv)] if hh == 0 else [],
                 inc=("oV4", 1) if hh == 1 else None)

    return [staging, fwdT, chain1, chain2, backT_evac]


def _onorm_plain_parts(p, ch, qg, pi, psB, A2, A2b, identf, identb, scl,
                       dst_ch, ap3):
    ALUx = ALU
    base = 1024 * pi
    bb = 2048 * pi
    stg8 = A2[:, 14400:15424]
    dqt = A2b[:, 30848:31360]
    psBb = psB[:].bitcast(BF16)

    def staging():
        for hh in range(2):
            p.op("vector", "tensor_copy", stg8[0:65, 512 * hh:512 * hh + 512],
                 psB[0:65, base + 512 * hh:base + 512 * hh + 512],
                 waits=[("aT2", p.val("aT2")), ("oT", p.val("oT"))]
                 if hh == 0 else [],
                 inc=("oV", 1) if hh == 1 else None)

    def fwdT():
        for t in range(4):
            for hh in range(2):
                i = 2 * t + hh
                off = base + 65 * i if i < 4 else base + 512 + 65 * (i - 4)
                p.op("tensor", "transpose", psB[:, off:off + 65],
                     stg8[0:65, 512 * hh + 128 * t:512 * hh + 128 * t + 128],
                     identf[0:65, 0:65],
                     waits=[("oV", p.val("oV"))] if i == 0 else [],
                     inc=("oT", 1) if i == 7 else None)

    def chain1():
        p.wait("vector", "oT", p.val("oT"))
        p.op("vector", "reciprocal", scl[:, 16:20],
             ap3(psB, base + 64, [[2048, 128], [65, 4], [1, 1]]), drain=True)
        p.op("vector", "reciprocal", scl[:, 20:24],
             ap3(psB, base + 512 + 64, [[2048, 128], [65, 4], [1, 1]]))
        rzc0 = ap3(scl, 16, [[64, 128], [1, 4], [0, 64]])
        rzc1 = ap3(scl, 20, [[64, 128], [1, 4], [0, 64]])
        p.op("vector", "tensor_tensor",
             dqt.rearrange("p (b c) -> p b c", b=8)[:, 0:4, :],
             ap3(psB, base, [[2048, 128], [65, 4], [1, 64]]), rzc0, ALUx.mult,
             drain=True, waits=[("oT2", p.val("oT2"))])
        p.op("vector", "tensor_tensor",
             dqt.rearrange("p (b c) -> p b c", b=8)[:, 4:8, :],
             ap3(psB, base + 512, [[2048, 128], [65, 4], [1, 64]]), rzc1,
             ALUx.mult, inc=("oV3", 1))

    def chain2():
        pass

    def backT_evac():
        for t in range(4):
            off = bb + 128 * t if t < 2 else bb + 1024 + 128 * (t - 2)
            p.op("tensor", "transpose", psBb[:, off:off + 128],
                 dqt[:, 128 * t:128 * t + 128], identb[:, :],
                 waits=[("oV3", p.val("oV3"))] if t == 0 else [],
                 inc=("oT2", 1) if t == 3 else None)
        otv = p.val("oT2")
        for hh in range(2):
            p.op("vector", "tensor_copy",
                 dst_ch(ch)[:, 512 * qg + 256 * hh:512 * qg + 256 * hh + 256],
                 psBb[:, bb + 1024 * hh:bb + 1024 * hh + 256],
                 waits=[("oT2", otv)] if hh == 0 else [],
                 inc=("oV4", 1) if hh == 1 else None)

    return [staging, fwdT, chain1, chain2, backT_evac]


def uv_sb(A2, ap3):
    return ap3(A2, 15424, [[16384, 128], [64, 8], [1, 64]])


def _emit_out_zero(p, out_t, A2):
    p.barrier()
    p.op("vector", "memset", A2[:, 0:1024], 0.0, drain=True, inc=("zz", 1))
    for tt in range(NTT):
        p.op("sync", "dma_start", out=out_t[128 * tt:128 * (tt + 1), :],
             in_=A2[:, 0:1024], waits=[("zz", p.val("zz"))], inc=("dO", 16))
    p.wait("sync", "dO", p.val("dO"))


_CACHE = {}


def _prep_inputs(inputs):
    c = make_consts()
    w = {}
    w["wqkv_r"] = colchunks(np.asarray(inputs["L0_attn_W_qkv"], np.float32), 24)
    w["bqkv_cols"] = np.ascontiguousarray(
        np.asarray(inputs["L0_attn_b_qkv"], np.float32).reshape(24, 128).T)
    w["wout_r"] = colchunks(np.asarray(inputs["L0_attn_W_out"], np.float32), 8)
    w["bout_cols"] = np.ascontiguousarray(
        np.asarray(inputs["L0_attn_b_out"], np.float32).reshape(8, 128).T)
    wffn = np.asarray(inputs["L0_ffn_weight"], np.float32)
    w["w_ffn"] = wffn
    w["w_ffnT"] = np.ascontiguousarray(wffn.T)
    for k, nm in (("wq", "L1_attn_wq"), ("wk", "L1_attn_wk"),
                  ("wv", "L1_attn_wv"), ("wo", "L1_attn_wo")):
        w[f"l1{k}_r"] = colchunks(np.asarray(inputs[nm], np.float32), 8)
    for k, nm in (("q", "L1_attn_bq"), ("k", "L1_attn_bk"),
                  ("v", "L1_attn_bv"), ("o", "L1_attn_bo")):
        w[f"b{k}_cols"] = np.ascontiguousarray(
            np.asarray(inputs[nm], np.float32).reshape(8, 128).T)
    w["wg_cols"] = np.ascontiguousarray(
        np.asarray(inputs["L1_ffn_Wg"], np.float32).reshape(8, 128).T)
    for k, nm in (("Wq", "L1_ffn_Wq"), ("Wk", "L1_ffn_Wk"),
                  ("Wv", "L1_ffn_Wv")):
        w[f"f{k}_r"] = colchunks(np.asarray(inputs[nm], np.float32), 8)
    w["fWo_nat"] = np.ascontiguousarray(
        np.asarray(inputs["L1_ffn_Wo"], np.float32))
    w.update(c)
    return w


def kernel(**inputs):
    key = "final"
    if key not in _CACHE:
        _CACHE[key] = build_program(stage=STAGE_FINAL)
    nc, dbg = _CACHE[key]
    shared = _prep_inputs(inputs)
    x = np.asarray(inputs["L0_attn_x"], np.float32)
    in_maps = []
    for b in range(8):
        m = dict(shared)
        m["x_in"] = np.ascontiguousarray(x[b])
        in_maps.append(m)
    res = run_bass_kernel_spmd(nc, in_maps, core_ids=list(range(8)))
    out = np.stack([res.results[b]["out"] for b in range(8)], axis=0)
    return out
